# version 14
# baseline (speedup 1.0000x reference)
import numpy as np
import ml_dtypes

try:
    import concourse.bass as bass
except ImportError:
    import sys
    sys.path.insert(0, "/opt/trn_rl_repo")
    import concourse.bass as bass
import concourse.mybir as mybir
from concourse.bass_utils import run_bass_kernel_spmd
from concourse.bass_core_rust import IndirectOffsetOnAxis
from contextlib import ExitStack

F32 = mybir.dt.float32
BF16 = mybir.dt.bfloat16
U32 = mybir.dt.uint32
AF = mybir.ActivationFunctionType
ALU = mybir.AluOpType
AX = mybir.AxisListType.X
BF = ml_dtypes.bfloat16

NCORES = 8
T = 1024
NTILE = 8
N = 4096
KB = 17
NBIN = 2176
TAUB = 8

ENGS = ("sync", "gpsimd", "vector", "scalar", "tensor")


class Prog:

    def __init__(self):
        self.items = {e: [] for e in ENGS}
        self.cnt = {}

    def add(self, eng, fn, deps=(), sem=None, amt=1):
        waits = {}
        for d in deps:
            if d is None:
                continue
            s, v = d
            if v > 0:
                waits[s] = max(waits.get(s, 0), v)
        tok = None
        if sem is not None:
            self.cnt[sem] = self.cnt.get(sem, 0) + amt
            tok = (sem, self.cnt[sem])
        self.items[eng].append((fn, waits, tok))
        return tok


_CONSTS = None


def _build_consts():
    global _CONSTS
    if _CONSTS is not None:
        return _CONSTS
    kk = np.arange(2049, dtype=np.float64)
    ang = 2.0 * np.pi * np.outer(kk, kk) / N
    Fc = np.cos(ang)
    Fc[0] *= 0.5
    Fc[2048] *= 0.5
    Fs = np.sin(ang)
    Fs[0] = 0.0
    Fs[2048] = 0.0
    Fcp = np.zeros((NBIN, NBIN), np.float64)
    Fcp[:2049, :2049] = Fc
    Fsp = np.zeros((NBIN, NBIN), np.float64)
    Fsp[:2049, :2049] = Fs
    cm = np.full(NBIN, -1, np.int64)
    cm[0:1025] = np.arange(0, 2049, 2)
    cm[1152:2176] = np.arange(1, 2048, 2)
    Fcp = Fcp[:, :2049][:, np.clip(cm, 0, None)] * (cm >= 0)[None, :]
    Fsp = Fsp[:, :2049][:, np.clip(cm, 0, None)] * (cm >= 0)[None, :]
    tau = np.arange(2048, dtype=np.float64)
    coef = np.where(cm == 0, 0.0, np.where(cm == 2048, 1.0 / N, 2.0 / N))
    coef = coef * (cm >= 0)
    Ccp = coef[:, None] * np.cos(2.0 * np.pi * np.outer(np.clip(cm, 0, None), tau) / N)
    Fc_h = np.ascontiguousarray(
        Fcp.reshape(KB, 128, KB, 128).transpose(2, 1, 0, 3).reshape(KB, 128, NBIN)
    ).astype(BF)
    Fs_h = np.ascontiguousarray(
        Fsp.reshape(KB, 128, KB, 128).transpose(2, 1, 0, 3).reshape(KB, 128, NBIN)
    ).astype(BF)
    Cc_h = np.ascontiguousarray(
        Ccp.reshape(KB, 128, 4, 512).transpose(2, 1, 0, 3).reshape(4, 128, KB * 512)
    ).astype(BF)
    ident = np.eye(128, dtype=np.float32)
    _CONSTS = (Fc_h, Fs_h, Cc_h, ident)
    return _CONSTS


_NC_CACHE = None
DEBUG = False


def build_nc():
    nc = bass.Bass("TRN2", target_bir_lowering=False, dynamic_dma_scratch_size=16384)

    x_tok = nc.dram_tensor("x_tok", [T, N], F32, kind="ExternalInput").ap()
    xf_in = nc.dram_tensor("xf_in", [NBIN, T], F32, kind="ExternalInput").ap()
    xr_in = nc.dram_tensor("xr_in", [NBIN, T], F32, kind="ExternalInput").ap()
    x_pf = nc.dram_tensor("x_pf", [N, T], F32, kind="ExternalInput").ap()
    up_h = nc.dram_tensor("up_h", [16, 128, 2048], F32, kind="ExternalInput").ap()
    down_h = nc.dram_tensor("down_h", [16, 128, 2048], F32, kind="ExternalInput").ap()
    perm_t = nc.dram_tensor("perm_t", [128, 32], U32, kind="ExternalInput").ap()
    rowb_in = nc.dram_tensor("rowb_in", [128, 8], U32, kind="ExternalInput").ap()
    fc_in = nc.dram_tensor("fc_in", [KB, 128, NBIN], BF16, kind="ExternalInput").ap()
    fs_in = nc.dram_tensor("fs_in", [KB, 128, NBIN], BF16, kind="ExternalInput").ap()
    cc_in = nc.dram_tensor("cc_in", [4, 128, KB * 512], BF16, kind="ExternalInput").ap()
    id_in = nc.dram_tensor("id_in", [128, 128], F32, kind="ExternalInput").ap()

    out_d = nc.dram_tensor("out_d", [T, N], F32, kind="ExternalOutput").ap()

    xn_dbl = nc.dram_tensor("xn_dbl", [T * 2 * N], BF16, kind="Internal").ap()
    xn_rows = xn_dbl.rearrange("(t f) -> t f", f=2 * N)
    xn_flat = xn_dbl.rearrange("(a b) -> a b", b=1)
    g_hbm = nc.dram_tensor("g_hbm", [N, T], BF16, kind="Internal").ap()

    es = ExitStack()
    sb = lambda name, shape, dt: es.enter_context(nc.sbuf_tensor(name, shape, dt)).ap()

    AR = sb("arena", [128, 51456], BF16)
    ARF = AR.bitcast(F32)
    P_sb = sb("p_sb", [128, KB * 1024], BF16)
    P_f32 = P_sb.bitcast(F32)
    S_sb = sb("s_sb", [128, NTILE * N], BF16)
    S_f32 = S_sb.bitcast(F32)
    stats = sb("stats", [128, 16], F32)
    msum8 = sb("msum8", [128, 8], F32)
    qcol8 = sb("qcol8", [128, 8], F32)
    tmp8b = sb("tmp8b", [128, 8], F32)
    tmp8c = sb("tmp8c", [128, 8], F32)
    tmp8d = sb("tmp8d", [128, 8], F32)
    rsqc = sb("rsqc", [128, 8], F32)
    nmrc = sb("nmrc", [128, 8], F32)
    t8v = sb("t8v", [128, 64], F32)
    t8i = sb("t8i", [128, 64], U32)
    idxw = sb("idxw", [128, 64], U32)
    wsm = sb("wsm", [128, 64], F32)
    we8 = sb("we8", [128, 8], F32)
    ex8 = sb("ex8", [128, 8], F32)
    sm1 = sb("sm1", [128, 1], F32)
    sm2 = sb("sm2", [128, 1], F32)
    rowb = sb("rowb", [128, 8], U32)
    perm_sb = sb("perm_sb", [128, 32], U32)
    ident_sb = sb("ident_sb", [128, 128], F32)
    ones1 = sb("ones1", [1, 128], F32)
    statsT = sb("statsT", [16, 128], F32)
    cvals = sb("cvals", [128, 2], F32)
    mrow = S_f32[0:1, 0:1024]
    rrow = S_f32[0:1, 1024:2048]
    ptmp = S_sb[:, 29696:30720]

    ps = [es.enter_context(nc.psum_tensor(f"ps{i}", [128, 512], F32)).ap() for i in range(8)]

    sems = {}
    for name in ("qS", "qCC0", "qCC1", "qXT", "qXN", "qROW", "qGX0", "qGX1",
                 "qLX0", "qLX1", "qST0", "qST1",
                 "qXF0", "qXF1", "qXR0", "qXR1", "qF0", "qF1",
                 "qUP0", "qUP1", "qDN0", "qDN1", "qXP0", "qXP1",
                 "qGA0", "qGA1", "qGA2", "qGA3", "qSC",
                 "eV", "eA", "eT"):
        sems[name] = es.enter_context(nc.semaphore(name))

    ccbuf = [S_sb[:, 0:KB * 512], S_sb[:, KB * 512:2 * KB * 512]]
    corr = [AR[:, t * N:(t + 1) * N] for t in range(NTILE)]

    pr = Prog()

    p0a = pr.add("sync", lambda e: e.dma_start(out=perm_sb, in_=perm_t), sem="qS", amt=16)
    p0b = pr.add("sync", lambda e: e.dma_start(out=rowb, in_=rowb_in), deps=[p0a], sem="qS", amt=16)
    p0c = pr.add("sync", lambda e: e.dma_start(out=ident_sb, in_=id_in), deps=[p0b], sem="qS", amt=16)
    c_one = pr.add("vector", lambda e: e.memset(ones1, 1.0), sem="eV")
    c_eps = pr.add("vector", lambda e: e.memset(cvals[:, 0:1], 1e-5), sem="eV")
    c_half = pr.add("vector", lambda e: e.memset(cvals[:, 1:2], -0.5), sem="eV")
    cc_load = [pr.add("sync", lambda e, tb=tb: e.dma_start(out=ccbuf[tb], in_=cc_in[tb]),
                      sem="qCC%d" % tb, amt=16) for tb in range(2)]

    xF = ARF[:, 21632:25728]
    dumpb = S_sb[:, 17408:21504]
    xnb = [S_sb[:, 21504 + j * N:21504 + (j + 1) * N] for j in range(2)]
    eP = AR[:, 0:KB * 1024]
    oP = AR[:, KB * 1024:2 * KB * 1024]
    xfb = [AR[:, 34816 + j * 1024:34816 + (j + 1) * 1024] for j in range(2)]
    xrb = [AR[:, 36864 + j * 1024:36864 + (j + 1) * 1024] for j in range(2)]
    fslot = [AR[:, 38912 + j * NBIN:38912 + (j + 1) * NBIN] for j in range(2)]

    xn_write_toks = []
    rstd_toks = [None] * NTILE
    rsq_toks = [None] * NTILE
    xn_prev = None
    sq_prev = None
    xF_free = None
    eo_last = [None, None]
    eo_final = None

    def emit_p1_tile(i):
        nonlocal xn_prev, sq_prev, xF_free
        ld = pr.add(
            "sync",
            lambda e, i=i: e.dma_start(out=xF, in_=x_tok[i * 128:(i + 1) * 128, :]),
            deps=[xF_free], sem="qXT", amt=16)
        rs = pr.add(
            "vector",
            lambda e, i=i: e.reduce_sum(msum8[:, i:i + 1], xF, axis=AX),
            deps=[ld], sem="eV")
        sq = pr.add(
            "vector",
            lambda e, i=i: e.scalar_tensor_tensor(
                dumpb, xF, 1.0, xF, ALU.mult, ALU.mult, accum_out=qcol8[:, i:i + 1]),
            deps=[ld, sq_prev], sem="eV")
        sq_prev = sq
        m = pr.add(
            "vector",
            lambda e, i=i: e.tensor_scalar(stats[:, i:i + 1], msum8[:, i:i + 1], 1.0 / N, None, ALU.mult),
            deps=[rs], sem="eV")
        t1 = pr.add(
            "vector",
            lambda e, i=i: e.tensor_mul(tmp8b[:, i:i + 1], stats[:, i:i + 1], stats[:, i:i + 1]),
            deps=[m], sem="eV")
        var = pr.add(
            "vector",
            lambda e, i=i: e.tensor_scalar(
                tmp8c[:, i:i + 1], qcol8[:, i:i + 1], 1.0 / N, tmp8b[:, i:i + 1],
                ALU.mult, ALU.subtract),
            deps=[sq, t1], sem="eV")
        lnv = pr.add(
            "scalar",
            lambda e, i=i: e.activation(out=tmp8d[:, i:i + 1], in_=tmp8c[:, i:i + 1],
                                        func=AF.Ln, bias=cvals[:, 0:1]),
            deps=[var, c_eps], sem="eA")
        rstd = pr.add(
            "scalar",
            lambda e, i=i: e.activation(out=stats[:, 8 + i:9 + i], in_=tmp8d[:, i:i + 1],
                                        func=AF.Exp, scale=cvals[:, 1:2]),
            deps=[lnv, c_half], sem="eA")
        rstd_toks[i] = rstd
        rsq_toks[i] = pr.add(
            "vector",
            lambda e, i=i: e.tensor_mul(rsqc[:, i:i + 1], stats[:, 8 + i:9 + i], stats[:, 8 + i:9 + i]),
            deps=[rstd], sem="eV")
        nmr = pr.add(
            "vector",
            lambda e, i=i: e.scalar_tensor_tensor(
                nmrc[:, i:i + 1], stats[:, i:i + 1], -1.0, stats[:, 8 + i:9 + i],
                ALU.mult, ALU.mult),
            deps=[m, rstd], sem="eV")
        b = i % 2
        xn = pr.add(
            "scalar",
            lambda e, b=b, i=i: e.activation(out=xnb[b], in_=xF, func=AF.Identity,
                                             bias=nmrc[:, i:i + 1], scale=stats[:, 8 + i:9 + i]),
            deps=[rstd, nmr, ld, xn_prev], sem="eA")
        xF_free = xn
        w1 = pr.add(
            "sync",
            lambda e, b=b, i=i: e.dma_start(out=xn_rows[i * 128:(i + 1) * 128, 0:N], in_=xnb[b]),
            deps=[xn], sem="qXN", amt=16)
        w2 = pr.add(
            "sync",
            lambda e, b=b, i=i: e.dma_start(out=xn_rows[i * 128:(i + 1) * 128, N:2 * N], in_=xnb[b]),
            deps=[xn], sem="qXN", amt=16)
        xn_write_toks.append(w2)
        xn_prev = w2

    def emit_p2_chunk(fc):
        nonlocal eo_final
        b = fc % 2
        lf = pr.add(
            "gpsimd",
            lambda e, b=b, fc=fc: e.dma_start(out=xfb[b], in_=xf_in[fc * 128:(fc + 1) * 128, :]),
            deps=[eo_last[b]], sem="qXF%d" % b, amt=16)
        lr = pr.add(
            "gpsimd",
            lambda e, b=b, fc=fc: e.dma_start(out=xrb[b], in_=xr_in[fc * 128:(fc + 1) * 128, :]),
            deps=[eo_last[b]], sem="qXR%d" % b, amt=16)
        te = pr.add(
            "vector",
            lambda e, b=b, fc=fc: e.tensor_add(eP[:, fc * 1024:(fc + 1) * 1024], xfb[b], xrb[b]),
            deps=[lf, lr], sem="eV")
        to = pr.add(
            "vector",
            lambda e, b=b, fc=fc: e.scalar_tensor_tensor(
                oP[:, fc * 1024:(fc + 1) * 1024], xfb[b], 2.0,
                eP[:, fc * 1024:(fc + 1) * 1024], ALU.mult, ALU.subtract),
            deps=[te], sem="eV")
        eo_last[b] = to
        eo_final = to

    for step in range(KB):
        emit_p2_chunk(step)

    f_cons_tok = [None, None]
    bank_mm = [None] * 8
    bank_ev = [None] * 8
    ptmp_war = [None, None]
    p_last = None
    si = 0
    for bb in range(KB):
        for pi, src in ((0, fc_in), (1, fs_in)):
            slot = si % 2
            si += 1
            fl = pr.add(
                "gpsimd",
                lambda e, slot=slot, src=src, bb=bb: e.dma_start(out=fslot[slot], in_=src[bb]),
                deps=[f_cons_tok[slot]], sem="qF%d" % slot, amt=16)
            p_c_tok = None
            for th in range(2):
                bank = ((bb * 2 + th) * 2 + pi) % 8
                eo = eP if pi == 0 else oP

                def mmgroup(e, slot=slot, bank=bank, eo=eo, th=th):
                    last = None
                    for fc in range(KB):
                        last = e.matmul(
                            ps[bank],
                            fslot[slot][:, fc * 128:(fc + 1) * 128],
                            eo[:, fc * 1024 + th * 512:fc * 1024 + th * 512 + 512],
                            start=(fc == 0), stop=(fc == KB - 1))
                    return last

                mg = pr.add("tensor", mmgroup,
                            deps=[fl, eo_final, bank_ev[bank]], sem="eT")
                bank_mm[bank] = mg
                f_cons_tok[slot] = mg
                dst = P_sb[:, bb * 1024 + th * 512: bb * 1024 + th * 512 + 512]
                if pi == 0:
                    ev = pr.add(
                        "scalar",
                        lambda e, bank=bank, dst=dst: e.activation(out=dst, in_=ps[bank], func=AF.Square),
                        deps=[mg], sem="eA")
                    bank_ev[bank] = ev
                    p_c_tok = ev
                else:
                    ev = pr.add(
                        "scalar",
                        lambda e, bank=bank, th=th: e.activation(
                            out=ptmp[:, th * 512:(th + 1) * 512], in_=ps[bank], func=AF.Square),
                        deps=[mg, ptmp_war[th]], sem="eA")
                    bank_ev[bank] = ev
                    pa = pr.add(
                        "vector",
                        lambda e, dst=dst, th=th: e.tensor_add(
                            dst, dst, ptmp[:, th * 512:(th + 1) * 512]),
                        deps=[ev, p_c_tok], sem="eV")
                    ptmp_war[th] = pa
                    p_last = pa
        if bb < NTILE:
            emit_p1_tile(bb)

    cc_cons = [None, None]
    tile_ev = [None] * NTILE
    ctx = {"p3_last": None, "grp0_p4a": None}

    p4a = {"prev_ex": None}
    find_toks = [None] * NTILE
    idx_toks = [None] * NTILE
    w_toks = [None] * NTILE

    def emit_p4a_tile(t):
        c = corr[t]
        mx = pr.add("vector",
                    lambda e, t=t, c=c: e.max(out=t8v[:, t * 8:(t + 1) * 8], in_=c),
                    deps=[tile_ev[t], rsq_toks[t]], sem="eV")
        fi = pr.add("vector",
                    lambda e, t=t, c=c: e.max_index(out=t8i[:, t * 8:(t + 1) * 8],
                                                    in_max=t8v[:, t * 8:(t + 1) * 8], in_values=c),
                    deps=[mx], sem="eV")
        find_toks[t] = fi
        wv = pr.add("vector",
                    lambda e, t=t: e.tensor_scalar(wsm[:, t * 8:(t + 1) * 8], t8v[:, t * 8:(t + 1) * 8],
                                                   rsqc[:, t:t + 1], None, ALU.mult),
                    deps=[mx], sem="eV")
        sub = pr.add("vector",
                     lambda e, t=t: e.tensor_scalar(we8, wsm[:, t * 8:(t + 1) * 8],
                                                    wsm[:, t * 8:t * 8 + 1], None, ALU.subtract),
                     deps=[wv, p4a["prev_ex"]], sem="eV")
        ex = pr.add("scalar",
                    lambda e: e.activation(out=ex8, in_=we8, func=AF.Exp),
                    deps=[sub], sem="eA")
        p4a["prev_ex"] = ex
        ssum = pr.add("vector", lambda e: e.reduce_sum(sm1, ex8, axis=AX), deps=[ex], sem="eV")
        rsum = pr.add("vector", lambda e: e.reciprocal(sm2, sm1), deps=[ssum], sem="eV")
        w_toks[t] = pr.add("vector",
                           lambda e, t=t: e.tensor_scalar(wsm[:, t * 8:(t + 1) * 8], ex8,
                                                          sm2[:, 0:1], None, ALU.mult),
                           deps=[rsum, sub], sem="eV")
        idx_toks[t] = pr.add(
            "vector",
            lambda e, t=t: e.tensor_tensor(idxw[:, t * 8:(t + 1) * 8], t8i[:, t * 8:(t + 1) * 8],
                                           rowb[:, t:t + 1].to_broadcast([128, 8]), ALU.add),
            deps=[find_toks[t], p0b], sem="eV")
        return idx_toks[t]

    roll = [AR[:, 32768 + j * N:32768 + (j + 1) * N] for j in range(4)]
    roll_cons = [None, None, None, None]
    fma_last = [None] * NTILE

    p4b_state = {}

    def emit_p4b_tile(t, mbdep=None, ks=None):
        fma_prev = p4b_state.get(t)
        if ks is None:
            ks = range(8)
        for k in ks:
            rb = k % 4
            ga = pr.add(
                "gpsimd",
                lambda e, t=t, k=k, rb=rb: e.indirect_dma_start(
                    out=roll[rb], out_offset=None, in_=xn_flat,
                    in_offset=IndirectOffsetOnAxis(idxw[:, t * 8 + k:t * 8 + k + 1], 0)),
                deps=[idx_toks[t], roll_cons[rb], xn_write_toks[-1]],
                sem="qGA%d" % rb, amt=16)
            st = S_sb[:, t * N:(t + 1) * N]
            if k == 0:
                extra = [ctx["p3_last"]] if t <= 4 else []
                if mbdep is not None:
                    extra.append(mbdep)
                fm = pr.add(
                    "vector",
                    lambda e, t=t, rb=rb, st=st: e.tensor_scalar(
                        st, roll[rb], wsm[:, t * 8:t * 8 + 1], None, ALU.mult),
                    deps=[ga, w_toks[t]] + extra, sem="eV")
            else:
                sc_ = pr.add(
                    "vector",
                    lambda e, t=t, k=k, rb=rb: e.tensor_scalar(
                        roll[rb], roll[rb], wsm[:, t * 8 + k:t * 8 + k + 1], None, ALU.mult),
                    deps=[ga, w_toks[t]], sem="eV")
                fm = pr.add(
                    "vector",
                    lambda e, rb=rb, st=st: e.tensor_add(st, st, roll[rb]),
                    deps=[sc_, fma_prev], sem="eV")
            fma_prev = fm
            roll_cons[rb] = fm
        p4b_state[t] = fma_prev
        fma_last[t] = fma_prev

    otmp = [S_sb[:, 17408 + j * 512:17408 + (j + 1) * 512] for j in range(2)]
    otmp_war = [None, None]
    bank_free = list(bank_ev)
    L = 0
    p3_mg = None
    for grp, tiles in ((0, (4, 5, 6, 7)), (1, (0, 1, 2, 3))):
        for tb in range(4):
            b = L % 2
            if L >= 2:
                cc_load[b] = pr.add(
                    "sync",
                    lambda e, b=b, tb=tb: e.dma_start(out=ccbuf[b], in_=cc_in[tb]),
                    deps=[cc_cons[b]], sem="qCC%d" % b, amt=16)
            L += 1
            for i, t in enumerate(tiles):
                bA, bB = 2 * i, 2 * i + 1

                def egroup(e, b=b, t=t, bA=bA):
                    last = None
                    for kc in range(9):
                        last = e.matmul(
                            ps[bA],
                            P_sb[:, kc * 1024 + t * 128:kc * 1024 + t * 128 + 128],
                            ccbuf[b][:, kc * 512:(kc + 1) * 512],
                            start=(kc == 0), stop=(kc == 8))
                    return last

                def ogroup(e, b=b, t=t, bB=bB):
                    last = None
                    for kc in range(9, KB):
                        last = e.matmul(
                            ps[bB],
                            P_sb[:, kc * 1024 + t * 128:kc * 1024 + t * 128 + 128],
                            ccbuf[b][:, kc * 512:(kc + 1) * 512],
                            start=(kc == 9), stop=(kc == KB - 1))
                    return last

                mge = pr.add("tensor", egroup,
                             deps=[cc_load[b], p_last, bank_free[bA]], sem="eT")
                mgo = pr.add("tensor", ogroup,
                             deps=[cc_load[b], p_last, bank_free[bB]], sem="eT")
                p3_mg = mgo
                cc_cons[b] = mgo
                ot = pr.add(
                    "scalar",
                    lambda e, bB=bB, ob=tb % 2: e.copy(out=otmp[ob], in_=ps[bB]),
                    deps=[mgo, otmp_war[tb % 2]], sem="eA")
                bank_free[bB] = ot
                lo = pr.add(
                    "vector",
                    lambda e, t=t, tb=tb, bA=bA, ob=tb % 2: e.tensor_add(
                        corr[t][:, tb * 512:(tb + 1) * 512], ps[bA], otmp[ob]),
                    deps=[mge, ot], sem="eV")
                hi = pr.add(
                    "vector",
                    lambda e, t=t, tb=tb, bA=bA, ob=tb % 2: e.tensor_sub(
                        corr[t][:, 2048 + tb * 512:2048 + (tb + 1) * 512], ps[bA], otmp[ob]),
                    deps=[lo], sem="eV")
                bank_free[bA] = hi
                otmp_war[tb % 2] = hi
                tile_ev[t] = hi
            if grp == 1:
                for (tt, kp) in {0: [(5, 0)], 1: [(5, 1), (5, 2)],
                                 2: [(5, 3), (6, 0), (6, 1)],
                                 3: [(6, 2), (6, 3), (7, 0)]}[tb]:
                    emit_p4b_tile(tt, ks=range(2 * kp, 2 * kp + 2))
        if grp == 0:
            for t in (5, 7, 4, 6):
                ctx["grp0_p4a"] = emit_p4a_tile(t)
    p3_last_mm = p3_mg
    ctx["p3_last"] = p3_last_mm
    grp0_p4a_done = ctx["grp0_p4a"]

    xnp = [P_sb[:, j * 1024:(j + 1) * 1024] for j in range(8)]
    xpb = [P_sb[:, 8192 + j * 1024:8192 + (j + 1) * 1024] for j in range(2)]
    upb = [P_sb[:, 10240 + j * 2048:10240 + (j + 1) * 2048] for j in range(2)]
    dnb = [AR[:, j * 2048:(j + 1) * 2048] for j in range(2)]
    Mb = P_sb[:, 14336:15360]
    Rb = P_sb[:, 15360:16384]
    t1g = [[AR[:, 16384 + hh * 8192 + mb * 1024:16384 + hh * 8192 + (mb + 1) * 1024]
            for mb in range(8)] for hh in range(2)]
    gbuf = [AR[:, 49152 + j * 1024:49152 + (j + 1) * 1024] for j in range(2)]

    up_loads = {}
    xpb_loads = {}
    for h in range(2):
        up_loads[h] = pr.add("gpsimd", lambda e, h=h: e.dma_start(out=upb[h % 2], in_=up_h[h]),
                             deps=[p3_last_mm], sem="qUP%d" % (h % 2), amt=16)
    for fc in range(2):
        xpb_loads[fc] = pr.add(
            "gpsimd",
            lambda e, fc=fc: e.dma_start(out=xpb[fc % 2], in_=x_pf[fc * 128:(fc + 1) * 128, :]),
            deps=[p3_last_mm], sem="qXP%d" % (fc % 2), amt=16)

    tr = pr.add("tensor",
                lambda e: e.transpose(ps[0][0:16, 0:128], stats[:, 0:16], ident_sb),
                deps=[bank_free[0], rstd_toks[-1], p0c], sem="eT")
    cpT = pr.add("vector",
                 lambda e: e.tensor_copy(statsT[0:16, :], ps[0][0:16, 0:128]),
                 deps=[tr], sem="eV")
    row_toks = []
    for j in range(8):
        row_toks.append(pr.add(
            "sync",
            lambda e, j=j: e.dma_start(out=mrow[0:1, j * 128:(j + 1) * 128], in_=statsT[j:j + 1, :]),
            deps=[cpT, p3_last_mm], sem="qROW", amt=16))
        row_toks.append(pr.add(
            "sync",
            lambda e, j=j: e.dma_start(out=rrow[0:1, j * 128:(j + 1) * 128], in_=statsT[8 + j:9 + j, :]),
            deps=[cpT, p3_last_mm], sem="qROW", amt=16))
    mb_toks = []
    for half in range(2):
        for ri, (rowap, dstap) in enumerate(((mrow, Mb), (rrow, Rb))):
            bank = 1 + half * 2 + ri
            mm = pr.add(
                "tensor",
                lambda e, bank=bank, rowap=rowap, half=half: e.matmul(
                    ps[bank], ones1[0:1, :], rowap[0:1, half * 512:(half + 1) * 512],
                    start=True, stop=True),
                deps=row_toks + [c_one, bank_free[bank]], sem="eT")
            mb_toks.append(pr.add(
                "scalar",
                lambda e, bank=bank, dstap=dstap, half=half: e.copy(
                    out=dstap[:, half * 512:(half + 1) * 512], in_=ps[bank]),
                deps=[mm], sem="eA"))

    p5 = {"up_cons": [None, None], "dn_cons": [None, None],
          "t1g_cons": [None, None], "gb_cons": [None, None],
          "bank_ev": [cpT, mb_toks[0], mb_toks[1], mb_toks[2], mb_toks[3],
                      bank_free[5], bank_free[6], bank_free[7]],
          "cnt": 0, "scatters": [], "xnp_tok": [None] * 8, "xnp_cons": [None] * 8,
          "xpb_cons": [None, None], "gelu": {}, "pend": None}

    def emit_p5_xnp(fc):
        b = fc % 2
        r = fc % 8
        if fc < 2:
            lx = xpb_loads[fc]
        else:
            lx = pr.add(
                "gpsimd",
                lambda e, b=b, fc=fc: e.dma_start(out=xpb[b], in_=x_pf[fc * 128:(fc + 1) * 128, :]),
                deps=[p5["xpb_cons"][b]], sem="qXP%d" % b, amt=16)
        s1 = pr.add(
            "vector",
            lambda e, b=b, r=r: e.tensor_sub(xnp[r], xpb[b], Mb),
            deps=[lx, p5["xnp_cons"][r], mb_toks[-1]], sem="eV")
        s2 = pr.add(
            "vector",
            lambda e, r=r: e.tensor_mul(xnp[r], xnp[r], Rb),
            deps=[s1], sem="eV")
        p5["xnp_tok"][r] = s2
        p5["xpb_cons"][b] = s1

    for fc0 in range(0, 6):
        emit_p5_xnp(fc0)

    def emit_p5_up(h):
        hb = h % 2
        if h < 2:
            lu = up_loads[h]
        else:
            lu = pr.add("gpsimd", lambda e, hb=hb, h=h: e.dma_start(out=upb[hb], in_=up_h[h]),
                        deps=[p5["up_cons"][hb]], sem="qUP%d" % hb, amt=16)
        gelu_toks = {}
        last_up = None
        for mb in range(8):
            for th in range(2):
                bank = p5["cnt"] % 8
                p5["cnt"] += 1

                def ugroup(e, hb=hb, h=h, mb=mb, th=th, bank=bank):
                    last = None
                    for kc in range(2):
                        last = e.matmul(
                            ps[bank],
                            upb[hb][:, kc * 1024 + mb * 128:kc * 1024 + mb * 128 + 128],
                            xnp[(2 * h + kc) % 8][:, th * 512:(th + 1) * 512],
                            start=(kc == 0), stop=(kc == 1))
                    return last

                mg = pr.add("tensor", ugroup,
                            deps=[lu, p5["xnp_tok"][(2 * h) % 8], p5["xnp_tok"][(2 * h + 1) % 8],
                                  p5["bank_ev"][bank], p5["t1g_cons"][hb]],
                            sem="eT")
                last_up = mg
                ge = pr.add(
                    "scalar",
                    lambda e, hb=hb, mb=mb, th=th, bank=bank: e.activation(
                        out=t1g[hb][mb][:, th * 512:(th + 1) * 512], in_=ps[bank], func=AF.Gelu),
                    deps=[mg, grp0_p4a_done], sem="eA")
                p5["bank_ev"][bank] = ge
                gelu_toks[(mb, th)] = ge
        p5["xnp_cons"][(2 * h) % 8] = last_up
        p5["xnp_cons"][(2 * h + 1) % 8] = last_up
        p5["up_cons"][hb] = last_up
        p5["gelu"][h] = gelu_toks

    def emit_p5_down(h):
        hb = h % 2
        if h < 2:
            ldn = pr.add("gpsimd", lambda e, hb=hb, h=h: e.dma_start(out=dnb[hb], in_=down_h[h]),
                         deps=[idx_toks[0], p3_last_mm], sem="qDN%d" % hb, amt=16)
        else:
            ldn = pr.add("gpsimd", lambda e, hb=hb, h=h: e.dma_start(out=dnb[hb], in_=down_h[h]),
                         deps=[p5["dn_cons"][hb]], sem="qDN%d" % hb, amt=16)
        gelu_toks = p5["gelu"].pop(h)
        gelu_max = max(gelu_toks.values(), key=lambda x: x[1])
        last_dn = None
        for db in range(2):
            gslot = db
            silu_toks = []
            for th in range(2):
                bank = p5["cnt"] % 8
                p5["cnt"] += 1

                def dgroup(e, hb=hb, db=db, th=th, bank=bank):
                    last = None
                    for mc in range(8):
                        last = e.matmul(
                            ps[bank],
                            dnb[hb][:, mc * 256 + db * 128:mc * 256 + db * 128 + 128],
                            t1g[hb][mc][:, th * 512:(th + 1) * 512],
                            start=(mc == 0), stop=(mc == 7))
                    return last

                mg = pr.add("tensor", dgroup,
                            deps=[ldn, p5["bank_ev"][bank], gelu_max, p5["gb_cons"][gslot]],
                            sem="eT")
                last_dn = mg
                si_ = pr.add(
                    "scalar",
                    lambda e, gslot=gslot, th=th, bank=bank: e.copy(
                        out=gbuf[gslot][:, th * 512:(th + 1) * 512], in_=ps[bank]),
                    deps=[mg], sem="eA")
                p5["bank_ev"][bank] = si_
                silu_toks.append(si_)
            sc = pr.add(
                "gpsimd",
                lambda e, gslot=gslot, h=h, db=db: e.indirect_dma_start(
                    out=g_hbm, out_offset=IndirectOffsetOnAxis(
                        perm_sb[:, 2 * h + db:2 * h + db + 1], 0),
                    in_=gbuf[gslot], in_offset=None),
                deps=silu_toks + [p0a, p5["scatters"][-1] if p5["scatters"] else None],
                sem="qSC", amt=16)
            p5["scatters"].append(sc)
            p5["gb_cons"][gslot] = sc
        p5["dn_cons"][hb] = last_dn
        p5["t1g_cons"][hb] = last_dn

    late_q = [(7, 1), (7, 2), (7, 3)]
    for tt in (4, 0, 1, 2, 3):
        late_q += [(tt, 0), (tt, 1), (tt, 2), (tt, 3)]
    late_sched = {}
    for i, q in enumerate(late_q):
        late_sched.setdefault(i // 2, []).append(q)
    for h in range(16):
        if 2 * h + 6 < 32:
            emit_p5_xnp(2 * h + 6)
        if 2 * h + 7 < 32:
            emit_p5_xnp(2 * h + 7)
        if h < 4:
            emit_p4a_tile(h)
        emit_p5_up(h)
        if h >= 1:
            emit_p5_down(h - 1)
        for (tt, kp) in late_sched.get(h, []):
            emit_p4b_tile(tt, mbdep=mb_toks[-1] if (tt == 0 and kp == 0) else None,
                          ks=range(2 * kp, 2 * kp + 2))
    emit_p5_down(15)
    p5_dn_last = p5["dn_cons"][15 % 2]
    scatter_last = p5["scatters"][-1]

    g_tok = [AR[:, j * N:(j + 1) * N] for j in range(2)]
    xtb = [ARF[:, 4096 + j * 4096:4096 + (j + 1) * 4096] for j in range(2)]
    sgb = [AR[:, 24576 + j * N:24576 + (j + 1) * N] for j in range(2)]
    g_cons = [None, None]
    xt_cons = [None, None]
    sg_cons = [None, None]
    for t in range(NTILE):
        b = t % 2
        gx = pr.add(
            "sync",
            lambda e, b=b, t=t: e.dma_start(out=g_tok[b], in_=g_hbm[:, t * 128:(t + 1) * 128],
                                            transpose=True),
            deps=[g_cons[b], scatter_last], sem="qGX%d" % b, amt=16)
        lx = pr.add(
            "sync",
            lambda e, b=b, t=t: e.dma_start(out=xtb[b], in_=x_tok[t * 128:(t + 1) * 128, :]),
            deps=[xt_cons[b], p5_dn_last], sem="qLX%d" % b, amt=16)
        sil = pr.add(
            "scalar",
            lambda e, b=b: e.activation(out=sgb[b], in_=g_tok[b], func=AF.Silu),
            deps=[gx, sg_cons[b]], sem="eA")
        g_cons[b] = sil
        sg = pr.add(
            "vector",
            lambda e, b=b, t=t: e.tensor_mul(sgb[b], S_sb[:, t * N:(t + 1) * N], sgb[b]),
            deps=[sil, fma_last[t]], sem="eV")
        ad = pr.add(
            "vector",
            lambda e, b=b: e.tensor_add(xtb[b], sgb[b], xtb[b]),
            deps=[sg, lx], sem="eV")
        sg_cons[b] = ad
        stk = pr.add(
            "sync",
            lambda e, b=b, t=t: e.dma_start(out=out_d[t * 128:(t + 1) * 128, :], in_=xtb[b]),
            deps=[ad], sem="qST%d" % b, amt=16)
        xt_cons[b] = stk

    if DEBUG:
        dbg = {}
        for nm, shape, dt in (("dbg_s", [128, NTILE * N], BF16), ("dbg_w", [128, 64], F32),
                              ("dbg_t8v", [128, 64], F32), ("dbg_idx", [128, 64], U32),
                              ("dbg_stats", [128, 16], F32), ("dbg_mb", [128, 1024], BF16),
                              ("dbg_rb", [128, 1024], BF16), ("dbg_g", [N, T], BF16)):
            dbg[nm] = nc.dram_tensor(nm, shape, dt, kind="ExternalOutput").ap()
        allv = [(s, v) for s, v in pr.cnt.items()]
        pr.add("sync", lambda e: e.dma_start(out=dbg["dbg_s"], in_=S_sb), deps=allv, sem="qS", amt=16)
        pr.add("sync", lambda e: e.dma_start(out=dbg["dbg_w"], in_=wsm), deps=allv, sem="qS", amt=16)
        pr.add("sync", lambda e: e.dma_start(out=dbg["dbg_t8v"], in_=t8v), deps=allv, sem="qS", amt=16)
        pr.add("sync", lambda e: e.dma_start(out=dbg["dbg_idx"], in_=idxw), deps=allv, sem="qS", amt=16)
        pr.add("sync", lambda e: e.dma_start(out=dbg["dbg_stats"], in_=stats), deps=allv, sem="qS", amt=16)
        pr.add("sync", lambda e: e.dma_start(out=dbg["dbg_mb"], in_=Mb), deps=allv, sem="qS", amt=16)
        pr.add("sync", lambda e: e.dma_start(out=dbg["dbg_rb"], in_=Rb), deps=allv, sem="qS", amt=16)
        pr.add("sync", lambda e: e.dma_start(out=dbg["dbg_g"], in_=g_hbm), deps=allv, sem="qS", amt=16)

    pr.add("sync", None, deps=[("qST0", pr.cnt["qST0"]), ("qST1", pr.cnt["qST1"])])

    with es:
        with nc.Block() as block:
            def emit(elist):
                def body(e):
                    for fn, waits, tok in elist:
                        for s in sorted(waits):
                            e.wait_ge(sems[s], waits[s])
                        if fn is None:
                            continue
                        ins = fn(e)
                        if tok is not None:
                            ins.then_inc(sems[tok[0]], 16 if tok[0].startswith("q") else 1)
                return body

            block.sync(emit(pr.items["sync"]))
            block.gpsimd(emit(pr.items["gpsimd"]))
            block.vector(emit(pr.items["vector"]))
            block.scalar(emit(pr.items["scalar"]))
            block.tensor(emit(pr.items["tensor"]))
    return nc


def kernel(L0_attn_x, L0_ffn_up, L0_ffn_down, L0_ffn_perm):
    global _NC_CACHE
    x = np.ascontiguousarray(np.asarray(L0_attn_x, np.float32).reshape(8192, N))
    up = np.asarray(L0_ffn_up, np.float32)
    down = np.asarray(L0_ffn_down, np.float32)
    perm = np.asarray(L0_ffn_perm).astype(np.int64)

    Fc_h, Fs_h, Cc_h, ident = _build_consts()
    up_h = np.ascontiguousarray(
        up.reshape(16, 2, 128, 1024).transpose(0, 2, 1, 3).reshape(16, 128, 2048))
    down_h = np.ascontiguousarray(
        down.reshape(16, 8, 128, 256).transpose(0, 2, 1, 3).reshape(16, 128, 2048))
    perm_t = np.ascontiguousarray(perm.reshape(32, 128).T).astype(np.uint32)
    rowb = ((np.arange(8)[None, :] * 128 + np.arange(128)[:, None]) * (2 * N)).astype(np.uint32)

    in_maps = []
    for c in range(NCORES):
        X = x[c * T:(c + 1) * T]
        Xt = np.ascontiguousarray(X.T)
        xf = np.zeros((NBIN, T), np.float32)
        xf[0:2049] = Xt[0:2049]
        xr = np.zeros((NBIN, T), np.float32)
        xr[0] = Xt[0]
        xr[1:2049] = Xt[4095:2047:-1]
        x_pfc = np.ascontiguousarray(Xt[perm])
        in_maps.append({
            "x_tok": X, "xf_in": xf, "xr_in": xr, "x_pf": x_pfc,
            "up_h": up_h, "down_h": down_h, "perm_t": perm_t, "rowb_in": rowb,
            "fc_in": Fc_h, "fs_in": Fs_h, "cc_in": Cc_h, "id_in": ident,
        })

    if _NC_CACHE is None:
        _NC_CACHE = build_nc()
    res = run_bass_kernel_spmd(_NC_CACHE, in_maps, core_ids=list(range(NCORES)))
    out = np.concatenate([np.asarray(r["out_d"], np.float32) for r in res.results], axis=0)
    return out.reshape(1, 8192, N)


# version 15
# speedup vs baseline: 1.0144x; 1.0144x over previous
import numpy as np
import ml_dtypes

try:
    import concourse.bass as bass
except ImportError:
    import sys
    sys.path.insert(0, "/opt/trn_rl_repo")
    import concourse.bass as bass
import concourse.mybir as mybir
from concourse.bass_utils import run_bass_kernel_spmd
from concourse.bass_core_rust import IndirectOffsetOnAxis
from contextlib import ExitStack

F32 = mybir.dt.float32
BF16 = mybir.dt.bfloat16
U32 = mybir.dt.uint32
AF = mybir.ActivationFunctionType
ALU = mybir.AluOpType
AX = mybir.AxisListType.X
BF = ml_dtypes.bfloat16

NCORES = 8
T = 1024
NTILE = 8
N = 4096
KB = 17
NBIN = 2176
TAUB = 8

ENGS = ("sync", "gpsimd", "vector", "scalar", "tensor")


class Prog:

    def __init__(self):
        self.items = {e: [] for e in ENGS}
        self.cnt = {}

    def add(self, eng, fn, deps=(), sem=None, amt=1):
        waits = {}
        for d in deps:
            if d is None:
                continue
            s, v = d
            if v > 0:
                waits[s] = max(waits.get(s, 0), v)
        tok = None
        if sem is not None:
            self.cnt[sem] = self.cnt.get(sem, 0) + amt
            tok = (sem, self.cnt[sem])
        self.items[eng].append((fn, waits, tok))
        return tok


_CONSTS = None


def _build_consts():
    global _CONSTS
    if _CONSTS is not None:
        return _CONSTS
    kk = np.arange(2049, dtype=np.float64)
    ang = 2.0 * np.pi * np.outer(kk, kk) / N
    Fc = np.cos(ang)
    Fc[0] *= 0.5
    Fc[2048] *= 0.5
    Fs = np.sin(ang)
    Fs[0] = 0.0
    Fs[2048] = 0.0
    Fcp = np.zeros((NBIN, NBIN), np.float64)
    Fcp[:2049, :2049] = Fc
    Fsp = np.zeros((NBIN, NBIN), np.float64)
    Fsp[:2049, :2049] = Fs
    cm = np.full(NBIN, -1, np.int64)
    cm[0:1025] = np.arange(0, 2049, 2)
    cm[1152:2176] = np.arange(1, 2048, 2)
    Fcp = Fcp[:, :2049][:, np.clip(cm, 0, None)] * (cm >= 0)[None, :]
    Fsp = Fsp[:, :2049][:, np.clip(cm, 0, None)] * (cm >= 0)[None, :]
    tau = np.arange(2048, dtype=np.float64)
    coef = np.where(cm == 0, 0.0, np.where(cm == 2048, 1.0 / N, 2.0 / N))
    coef = coef * (cm >= 0)
    Ccp = coef[:, None] * np.cos(2.0 * np.pi * np.outer(np.clip(cm, 0, None), tau) / N)
    Fc_h = np.ascontiguousarray(
        Fcp.reshape(KB, 128, KB, 128).transpose(2, 1, 0, 3).reshape(KB, 128, NBIN)
    ).astype(BF)
    Fs_h = np.ascontiguousarray(
        Fsp.reshape(KB, 128, KB, 128).transpose(2, 1, 0, 3).reshape(KB, 128, NBIN)
    ).astype(BF)
    Cc_h = np.ascontiguousarray(
        Ccp.reshape(KB, 128, 4, 512).transpose(2, 1, 0, 3).reshape(4, 128, KB * 512)
    ).astype(BF)
    ident = np.eye(128, dtype=np.float32)
    _CONSTS = (Fc_h, Fs_h, Cc_h, ident)
    return _CONSTS


_NC_CACHE = None
DEBUG = False


def build_nc():
    nc = bass.Bass("TRN2", target_bir_lowering=False, dynamic_dma_scratch_size=16384)

    x_tok = nc.dram_tensor("x_tok", [T, N], F32, kind="ExternalInput").ap()
    xf_in = nc.dram_tensor("xf_in", [NBIN, T], F32, kind="ExternalInput").ap()
    xr_in = nc.dram_tensor("xr_in", [NBIN, T], F32, kind="ExternalInput").ap()
    x_pf = nc.dram_tensor("x_pf", [N, T], F32, kind="ExternalInput").ap()
    up_h = nc.dram_tensor("up_h", [16, 128, 2048], F32, kind="ExternalInput").ap()
    down_h = nc.dram_tensor("down_h", [16, 128, 2048], F32, kind="ExternalInput").ap()
    perm_t = nc.dram_tensor("perm_t", [128, 32], U32, kind="ExternalInput").ap()
    rowb_in = nc.dram_tensor("rowb_in", [128, 8], U32, kind="ExternalInput").ap()
    fc_in = nc.dram_tensor("fc_in", [KB, 128, NBIN], BF16, kind="ExternalInput").ap()
    fs_in = nc.dram_tensor("fs_in", [KB, 128, NBIN], BF16, kind="ExternalInput").ap()
    cc_in = nc.dram_tensor("cc_in", [4, 128, KB * 512], BF16, kind="ExternalInput").ap()
    id_in = nc.dram_tensor("id_in", [128, 128], F32, kind="ExternalInput").ap()

    out_d = nc.dram_tensor("out_d", [T, N], F32, kind="ExternalOutput").ap()

    xn_dbl = nc.dram_tensor("xn_dbl", [T * 2 * N], BF16, kind="Internal").ap()
    xn_rows = xn_dbl.rearrange("(t f) -> t f", f=2 * N)
    xn_flat = xn_dbl.rearrange("(a b) -> a b", b=1)
    g_hbm = nc.dram_tensor("g_hbm", [N, T], BF16, kind="Internal").ap()

    es = ExitStack()
    sb = lambda name, shape, dt: es.enter_context(nc.sbuf_tensor(name, shape, dt)).ap()

    AR = sb("arena", [128, 51456], BF16)
    ARF = AR.bitcast(F32)
    P_sb = sb("p_sb", [128, KB * 1024], BF16)
    P_f32 = P_sb.bitcast(F32)
    S_sb = sb("s_sb", [128, NTILE * N], BF16)
    S_f32 = S_sb.bitcast(F32)
    stats = sb("stats", [128, 16], F32)
    msum8 = sb("msum8", [128, 8], F32)
    qcol8 = sb("qcol8", [128, 8], F32)
    tmp8b = sb("tmp8b", [128, 8], F32)
    tmp8c = sb("tmp8c", [128, 8], F32)
    tmp8d = sb("tmp8d", [128, 8], F32)
    rsqc = sb("rsqc", [128, 8], F32)
    nmrc = sb("nmrc", [128, 8], F32)
    t8v = sb("t8v", [128, 64], F32)
    t8i = sb("t8i", [128, 64], U32)
    idxw = sb("idxw", [128, 64], U32)
    wsm = sb("wsm", [128, 64], F32)
    we8 = sb("we8", [128, 8], F32)
    ex8 = sb("ex8", [128, 8], F32)
    sm1 = sb("sm1", [128, 1], F32)
    sm2 = sb("sm2", [128, 1], F32)
    rowb = sb("rowb", [128, 8], U32)
    perm_sb = sb("perm_sb", [128, 32], U32)
    ident_sb = sb("ident_sb", [128, 128], F32)
    ones1 = sb("ones1", [1, 128], F32)
    statsT = sb("statsT", [16, 128], F32)
    cvals = sb("cvals", [128, 2], F32)
    mrow = S_f32[0:1, 0:1024]
    rrow = S_f32[0:1, 1024:2048]
    ptmp = S_sb[:, 29696:30720]

    ps = [es.enter_context(nc.psum_tensor(f"ps{i}", [128, 512], F32)).ap() for i in range(8)]

    sems = {}
    for name in ("qS", "qCC0", "qCC1", "qXT", "qXN", "qROW", "qGX0", "qGX1",
                 "qLX0", "qLX1", "qST0", "qST1", "qOX", "qAC0", "qAC1",
                 "qXF0", "qXF1", "qXR0", "qXR1", "qF0", "qF1",
                 "qUP0", "qUP1", "qDN0", "qDN1", "qXP0", "qXP1",
                 "qGA0", "qGA1", "qGA2", "qGA3", "qSC",
                 "eV", "eA", "eT"):
        sems[name] = es.enter_context(nc.semaphore(name))

    ccbuf = [S_sb[:, 0:KB * 512], S_sb[:, KB * 512:2 * KB * 512]]
    corr = [AR[:, t * N:(t + 1) * N] for t in range(NTILE)]

    pr = Prog()

    p0a = pr.add("sync", lambda e: e.dma_start(out=perm_sb, in_=perm_t), sem="qS", amt=16)
    p0b = pr.add("sync", lambda e: e.dma_start(out=rowb, in_=rowb_in), deps=[p0a], sem="qS", amt=16)
    p0c = pr.add("sync", lambda e: e.dma_start(out=ident_sb, in_=id_in), deps=[p0b], sem="qS", amt=16)
    c_one = pr.add("vector", lambda e: e.memset(ones1, 1.0), sem="eV")
    c_eps = pr.add("vector", lambda e: e.memset(cvals[:, 0:1], 1e-5), sem="eV")
    c_half = pr.add("vector", lambda e: e.memset(cvals[:, 1:2], -0.5), sem="eV")
    cc_load = [pr.add("sync", lambda e, tb=tb: e.dma_start(out=ccbuf[tb], in_=cc_in[tb]),
                      sem="qCC%d" % tb, amt=16) for tb in range(2)]

    xF = ARF[:, 21632:25728]
    dumpb = S_sb[:, 17408:21504]
    xnb = [S_sb[:, 21504 + j * N:21504 + (j + 1) * N] for j in range(2)]
    eP = AR[:, 0:KB * 1024]
    oP = AR[:, KB * 1024:2 * KB * 1024]
    xfb = [AR[:, 34816 + j * 1024:34816 + (j + 1) * 1024] for j in range(2)]
    xrb = [AR[:, 36864 + j * 1024:36864 + (j + 1) * 1024] for j in range(2)]
    fslot = [AR[:, 38912 + j * NBIN:38912 + (j + 1) * NBIN] for j in range(2)]

    xn_write_toks = []
    rstd_toks = [None] * NTILE
    rsq_toks = [None] * NTILE
    xn_prev = None
    sq_prev = None
    xF_free = None
    eo_last = [None, None]
    eo_final = None

    def emit_p1_tile(i):
        nonlocal xn_prev, sq_prev, xF_free
        ld = pr.add(
            "sync",
            lambda e, i=i: e.dma_start(out=xF, in_=x_tok[i * 128:(i + 1) * 128, :]),
            deps=[xF_free, getattr(pr, "last_ox", None)], sem="qXT", amt=16)
        pr.last_ox = pr.add(
            "sync",
            lambda e, i=i: e.dma_start(out=out_d[i * 128:(i + 1) * 128, :], in_=xF),
            deps=[ld], sem="qOX", amt=16)
        rs = pr.add(
            "vector",
            lambda e, i=i: e.reduce_sum(msum8[:, i:i + 1], xF, axis=AX),
            deps=[ld], sem="eV")
        sq = pr.add(
            "vector",
            lambda e, i=i: e.scalar_tensor_tensor(
                dumpb, xF, 1.0, xF, ALU.mult, ALU.mult, accum_out=qcol8[:, i:i + 1]),
            deps=[ld, sq_prev], sem="eV")
        sq_prev = sq
        m = pr.add(
            "vector",
            lambda e, i=i: e.tensor_scalar(stats[:, i:i + 1], msum8[:, i:i + 1], 1.0 / N, None, ALU.mult),
            deps=[rs], sem="eV")
        t1 = pr.add(
            "vector",
            lambda e, i=i: e.tensor_mul(tmp8b[:, i:i + 1], stats[:, i:i + 1], stats[:, i:i + 1]),
            deps=[m], sem="eV")
        var = pr.add(
            "vector",
            lambda e, i=i: e.tensor_scalar(
                tmp8c[:, i:i + 1], qcol8[:, i:i + 1], 1.0 / N, tmp8b[:, i:i + 1],
                ALU.mult, ALU.subtract),
            deps=[sq, t1], sem="eV")
        lnv = pr.add(
            "scalar",
            lambda e, i=i: e.activation(out=tmp8d[:, i:i + 1], in_=tmp8c[:, i:i + 1],
                                        func=AF.Ln, bias=cvals[:, 0:1]),
            deps=[var, c_eps], sem="eA")
        rstd = pr.add(
            "scalar",
            lambda e, i=i: e.activation(out=stats[:, 8 + i:9 + i], in_=tmp8d[:, i:i + 1],
                                        func=AF.Exp, scale=cvals[:, 1:2]),
            deps=[lnv, c_half], sem="eA")
        rstd_toks[i] = rstd
        rsq_toks[i] = pr.add(
            "vector",
            lambda e, i=i: e.tensor_mul(rsqc[:, i:i + 1], stats[:, 8 + i:9 + i], stats[:, 8 + i:9 + i]),
            deps=[rstd], sem="eV")
        nmr = pr.add(
            "vector",
            lambda e, i=i: e.scalar_tensor_tensor(
                nmrc[:, i:i + 1], stats[:, i:i + 1], -1.0, stats[:, 8 + i:9 + i],
                ALU.mult, ALU.mult),
            deps=[m, rstd], sem="eV")
        b = i % 2
        xn = pr.add(
            "scalar",
            lambda e, b=b, i=i: e.activation(out=xnb[b], in_=xF, func=AF.Identity,
                                             bias=nmrc[:, i:i + 1], scale=stats[:, 8 + i:9 + i]),
            deps=[rstd, nmr, ld, xn_prev], sem="eA")
        xF_free = xn
        w1 = pr.add(
            "sync",
            lambda e, b=b, i=i: e.dma_start(out=xn_rows[i * 128:(i + 1) * 128, 0:N], in_=xnb[b]),
            deps=[xn], sem="qXN", amt=16)
        w2 = pr.add(
            "sync",
            lambda e, b=b, i=i: e.dma_start(out=xn_rows[i * 128:(i + 1) * 128, N:2 * N], in_=xnb[b]),
            deps=[xn], sem="qXN", amt=16)
        xn_write_toks.append(w2)
        xn_prev = w2

    def emit_p2_chunk(fc):
        nonlocal eo_final
        b = fc % 2
        lf = pr.add(
            "gpsimd",
            lambda e, b=b, fc=fc: e.dma_start(out=xfb[b], in_=xf_in[fc * 128:(fc + 1) * 128, :]),
            deps=[eo_last[b]], sem="qXF%d" % b, amt=16)
        lr = pr.add(
            "gpsimd",
            lambda e, b=b, fc=fc: e.dma_start(out=xrb[b], in_=xr_in[fc * 128:(fc + 1) * 128, :]),
            deps=[eo_last[b]], sem="qXR%d" % b, amt=16)
        te = pr.add(
            "vector",
            lambda e, b=b, fc=fc: e.tensor_add(eP[:, fc * 1024:(fc + 1) * 1024], xfb[b], xrb[b]),
            deps=[lf, lr], sem="eV")
        to = pr.add(
            "vector",
            lambda e, b=b, fc=fc: e.scalar_tensor_tensor(
                oP[:, fc * 1024:(fc + 1) * 1024], xfb[b], 2.0,
                eP[:, fc * 1024:(fc + 1) * 1024], ALU.mult, ALU.subtract),
            deps=[te], sem="eV")
        eo_last[b] = to
        eo_final = to

    for step in range(KB):
        emit_p2_chunk(step)

    f_cons_tok = [None, None]
    bank_mm = [None] * 8
    bank_ev = [None] * 8
    ptmp_war = [None, None]
    p_last = None
    si = 0
    for bb in range(KB):
        for pi, src in ((0, fc_in), (1, fs_in)):
            slot = si % 2
            si += 1
            fl = pr.add(
                "gpsimd",
                lambda e, slot=slot, src=src, bb=bb: e.dma_start(out=fslot[slot], in_=src[bb]),
                deps=[f_cons_tok[slot]], sem="qF%d" % slot, amt=16)
            p_c_tok = None
            for th in range(2):
                bank = ((bb * 2 + th) * 2 + pi) % 8
                eo = eP if pi == 0 else oP

                def mmgroup(e, slot=slot, bank=bank, eo=eo, th=th):
                    last = None
                    for fc in range(KB):
                        last = e.matmul(
                            ps[bank],
                            fslot[slot][:, fc * 128:(fc + 1) * 128],
                            eo[:, fc * 1024 + th * 512:fc * 1024 + th * 512 + 512],
                            start=(fc == 0), stop=(fc == KB - 1))
                    return last

                mg = pr.add("tensor", mmgroup,
                            deps=[fl, eo_final, bank_ev[bank]], sem="eT")
                bank_mm[bank] = mg
                f_cons_tok[slot] = mg
                dst = P_sb[:, bb * 1024 + th * 512: bb * 1024 + th * 512 + 512]
                if pi == 0:
                    ev = pr.add(
                        "scalar",
                        lambda e, bank=bank, dst=dst: e.activation(out=dst, in_=ps[bank], func=AF.Square),
                        deps=[mg], sem="eA")
                    bank_ev[bank] = ev
                    p_c_tok = ev
                else:
                    ev = pr.add(
                        "scalar",
                        lambda e, bank=bank, th=th: e.activation(
                            out=ptmp[:, th * 512:(th + 1) * 512], in_=ps[bank], func=AF.Square),
                        deps=[mg, ptmp_war[th]], sem="eA")
                    bank_ev[bank] = ev
                    pa = pr.add(
                        "vector",
                        lambda e, dst=dst, th=th: e.tensor_add(
                            dst, dst, ptmp[:, th * 512:(th + 1) * 512]),
                        deps=[ev, p_c_tok], sem="eV")
                    ptmp_war[th] = pa
                    p_last = pa
        if bb < NTILE:
            emit_p1_tile(bb)

    cc_cons = [None, None]
    tile_ev = [None] * NTILE
    ctx = {"p3_last": None, "grp0_p4a": None}

    p4a = {"prev_ex": None}
    find_toks = [None] * NTILE
    idx_toks = [None] * NTILE
    w_toks = [None] * NTILE

    def emit_p4a_tile(t):
        c = corr[t]
        mx = pr.add("vector",
                    lambda e, t=t, c=c: e.max(out=t8v[:, t * 8:(t + 1) * 8], in_=c),
                    deps=[tile_ev[t], rsq_toks[t]], sem="eV")
        fi = pr.add("vector",
                    lambda e, t=t, c=c: e.max_index(out=t8i[:, t * 8:(t + 1) * 8],
                                                    in_max=t8v[:, t * 8:(t + 1) * 8], in_values=c),
                    deps=[mx], sem="eV")
        find_toks[t] = fi
        wv = pr.add("vector",
                    lambda e, t=t: e.tensor_scalar(wsm[:, t * 8:(t + 1) * 8], t8v[:, t * 8:(t + 1) * 8],
                                                   rsqc[:, t:t + 1], None, ALU.mult),
                    deps=[mx], sem="eV")
        sub = pr.add("vector",
                     lambda e, t=t: e.tensor_scalar(we8, wsm[:, t * 8:(t + 1) * 8],
                                                    wsm[:, t * 8:t * 8 + 1], None, ALU.subtract),
                     deps=[wv, p4a["prev_ex"]], sem="eV")
        ex = pr.add("scalar",
                    lambda e: e.activation(out=ex8, in_=we8, func=AF.Exp),
                    deps=[sub], sem="eA")
        p4a["prev_ex"] = ex
        ssum = pr.add("vector", lambda e: e.reduce_sum(sm1, ex8, axis=AX), deps=[ex], sem="eV")
        rsum = pr.add("vector", lambda e: e.reciprocal(sm2, sm1), deps=[ssum], sem="eV")
        w_toks[t] = pr.add("vector",
                           lambda e, t=t: e.tensor_scalar(wsm[:, t * 8:(t + 1) * 8], ex8,
                                                          sm2[:, 0:1], None, ALU.mult),
                           deps=[rsum, sub], sem="eV")
        idx_toks[t] = pr.add(
            "vector",
            lambda e, t=t: e.tensor_tensor(idxw[:, t * 8:(t + 1) * 8], t8i[:, t * 8:(t + 1) * 8],
                                           rowb[:, t:t + 1].to_broadcast([128, 8]), ALU.add),
            deps=[find_toks[t], p0b], sem="eV")
        return idx_toks[t]

    roll = [AR[:, 32768 + j * N:32768 + (j + 1) * N] for j in range(4)]
    roll_cons = [None, None, None, None]
    fma_last = [None] * NTILE

    def emit_p4b_tile(t, mbdep=None):
        fma_prev = None
        for k in range(8):
            rb = k % 4
            ga = pr.add(
                "gpsimd",
                lambda e, t=t, k=k, rb=rb: e.indirect_dma_start(
                    out=roll[rb], out_offset=None, in_=xn_flat,
                    in_offset=IndirectOffsetOnAxis(idxw[:, t * 8 + k:t * 8 + k + 1], 0)),
                deps=[idx_toks[t], roll_cons[rb], xn_write_toks[-1]],
                sem="qGA%d" % rb, amt=16)
            st = S_sb[:, t * N:(t + 1) * N]
            if k == 0:
                extra = [ctx["p3_last"]] if t <= 4 else []
                if mbdep is not None:
                    extra.append(mbdep)
                fm = pr.add(
                    "vector",
                    lambda e, t=t, rb=rb, st=st: e.tensor_scalar(
                        st, roll[rb], wsm[:, t * 8:t * 8 + 1], None, ALU.mult),
                    deps=[ga, w_toks[t]] + extra, sem="eV")
            else:
                sc_ = pr.add(
                    "vector",
                    lambda e, t=t, k=k, rb=rb: e.tensor_scalar(
                        roll[rb], roll[rb], wsm[:, t * 8 + k:t * 8 + k + 1], None, ALU.mult),
                    deps=[ga, w_toks[t]], sem="eV")
                fm = pr.add(
                    "vector",
                    lambda e, rb=rb, st=st: e.tensor_add(st, st, roll[rb]),
                    deps=[sc_, fma_prev], sem="eV")
            fma_prev = fm
            roll_cons[rb] = fm
        fma_last[t] = fma_prev

    otmp = [S_sb[:, 17408 + j * 512:17408 + (j + 1) * 512] for j in range(2)]
    otmp_war = [None, None]
    bank_free = list(bank_ev)
    L = 0
    p3_mg = None
    for grp, tiles in ((0, (4, 5, 6, 7)), (1, (0, 1, 2, 3))):
        for tb in range(4):
            b = L % 2
            if L >= 2:
                cc_load[b] = pr.add(
                    "sync",
                    lambda e, b=b, tb=tb: e.dma_start(out=ccbuf[b], in_=cc_in[tb]),
                    deps=[cc_cons[b]], sem="qCC%d" % b, amt=16)
            L += 1
            for i, t in enumerate(tiles):
                bA, bB = 2 * i, 2 * i + 1

                def egroup(e, b=b, t=t, bA=bA):
                    last = None
                    for kc in range(9):
                        last = e.matmul(
                            ps[bA],
                            P_sb[:, kc * 1024 + t * 128:kc * 1024 + t * 128 + 128],
                            ccbuf[b][:, kc * 512:(kc + 1) * 512],
                            start=(kc == 0), stop=(kc == 8))
                    return last

                def ogroup(e, b=b, t=t, bB=bB):
                    last = None
                    for kc in range(9, KB):
                        last = e.matmul(
                            ps[bB],
                            P_sb[:, kc * 1024 + t * 128:kc * 1024 + t * 128 + 128],
                            ccbuf[b][:, kc * 512:(kc + 1) * 512],
                            start=(kc == 9), stop=(kc == KB - 1))
                    return last

                mge = pr.add("tensor", egroup,
                             deps=[cc_load[b], p_last, bank_free[bA]], sem="eT")
                mgo = pr.add("tensor", ogroup,
                             deps=[cc_load[b], p_last, bank_free[bB]], sem="eT")
                p3_mg = mgo
                cc_cons[b] = mgo
                ot = pr.add(
                    "scalar",
                    lambda e, bB=bB, ob=tb % 2: e.copy(out=otmp[ob], in_=ps[bB]),
                    deps=[mgo, otmp_war[tb % 2]], sem="eA")
                bank_free[bB] = ot
                lo = pr.add(
                    "vector",
                    lambda e, t=t, tb=tb, bA=bA, ob=tb % 2: e.tensor_add(
                        corr[t][:, tb * 512:(tb + 1) * 512], ps[bA], otmp[ob]),
                    deps=[mge, ot], sem="eV")
                hi = pr.add(
                    "vector",
                    lambda e, t=t, tb=tb, bA=bA, ob=tb % 2: e.tensor_sub(
                        corr[t][:, 2048 + tb * 512:2048 + (tb + 1) * 512], ps[bA], otmp[ob]),
                    deps=[lo], sem="eV")
                bank_free[bA] = hi
                otmp_war[tb % 2] = hi
                tile_ev[t] = hi
            if grp == 1 and tb in (1, 2, 3):
                emit_p4b_tile({1: 5, 2: 6, 3: 7}[tb])
        if grp == 0:
            for t in (5, 7, 4, 6):
                ctx["grp0_p4a"] = emit_p4a_tile(t)
    p3_last_mm = p3_mg
    ctx["p3_last"] = p3_last_mm
    grp0_p4a_done = ctx["grp0_p4a"]

    xnp = [P_sb[:, j * 1024:(j + 1) * 1024] for j in range(8)]
    xpb = [P_sb[:, 8192 + j * 1024:8192 + (j + 1) * 1024] for j in range(2)]
    upb = [P_sb[:, 10240 + j * 2048:10240 + (j + 1) * 2048] for j in range(2)]
    dnb = [AR[:, j * 2048:(j + 1) * 2048] for j in range(2)]
    Mb = P_sb[:, 14336:15360]
    Rb = P_sb[:, 15360:16384]
    t1g = [[AR[:, 16384 + hh * 8192 + mb * 1024:16384 + hh * 8192 + (mb + 1) * 1024]
            for mb in range(8)] for hh in range(2)]
    gbuf = [AR[:, 49152 + j * 1024:49152 + (j + 1) * 1024] for j in range(2)]

    up_loads = {}
    xpb_loads = {}
    for h in range(2):
        up_loads[h] = pr.add("gpsimd", lambda e, h=h: e.dma_start(out=upb[h % 2], in_=up_h[h]),
                             deps=[p3_last_mm], sem="qUP%d" % (h % 2), amt=16)
    for fc in range(2):
        xpb_loads[fc] = pr.add(
            "gpsimd",
            lambda e, fc=fc: e.dma_start(out=xpb[fc % 2], in_=x_pf[fc * 128:(fc + 1) * 128, :]),
            deps=[p3_last_mm], sem="qXP%d" % (fc % 2), amt=16)

    tr = pr.add("tensor",
                lambda e: e.transpose(ps[0][0:16, 0:128], stats[:, 0:16], ident_sb),
                deps=[bank_free[0], rstd_toks[-1], p0c], sem="eT")
    cpT = pr.add("vector",
                 lambda e: e.tensor_copy(statsT[0:16, :], ps[0][0:16, 0:128]),
                 deps=[tr], sem="eV")
    row_toks = []
    for j in range(8):
        row_toks.append(pr.add(
            "sync",
            lambda e, j=j: e.dma_start(out=mrow[0:1, j * 128:(j + 1) * 128], in_=statsT[j:j + 1, :]),
            deps=[cpT, p3_last_mm], sem="qROW", amt=16))
        row_toks.append(pr.add(
            "sync",
            lambda e, j=j: e.dma_start(out=rrow[0:1, j * 128:(j + 1) * 128], in_=statsT[8 + j:9 + j, :]),
            deps=[cpT, p3_last_mm], sem="qROW", amt=16))
    mb_toks = []
    for half in range(2):
        for ri, (rowap, dstap) in enumerate(((mrow, Mb), (rrow, Rb))):
            bank = 1 + half * 2 + ri
            mm = pr.add(
                "tensor",
                lambda e, bank=bank, rowap=rowap, half=half: e.matmul(
                    ps[bank], ones1[0:1, :], rowap[0:1, half * 512:(half + 1) * 512],
                    start=True, stop=True),
                deps=row_toks + [c_one, bank_free[bank]], sem="eT")
            mb_toks.append(pr.add(
                "scalar",
                lambda e, bank=bank, dstap=dstap, half=half: e.copy(
                    out=dstap[:, half * 512:(half + 1) * 512], in_=ps[bank]),
                deps=[mm], sem="eA"))

    p5 = {"up_cons": [None, None], "dn_cons": [None, None],
          "t1g_cons": [None, None], "gb_cons": [None, None],
          "bank_ev": [cpT, mb_toks[0], mb_toks[1], mb_toks[2], mb_toks[3],
                      bank_free[5], bank_free[6], bank_free[7]],
          "cnt": 0, "scatters": [], "xnp_tok": [None] * 8, "xnp_cons": [None] * 8,
          "xpb_cons": [None, None], "gelu": {}, "pend": None}

    def emit_p5_xnp(fc):
        b = fc % 2
        r = fc % 8
        if fc < 2:
            lx = xpb_loads[fc]
        else:
            lx = pr.add(
                "gpsimd",
                lambda e, b=b, fc=fc: e.dma_start(out=xpb[b], in_=x_pf[fc * 128:(fc + 1) * 128, :]),
                deps=[p5["xpb_cons"][b]], sem="qXP%d" % b, amt=16)
        s1 = pr.add(
            "vector",
            lambda e, b=b, r=r: e.tensor_sub(xnp[r], xpb[b], Mb),
            deps=[lx, p5["xnp_cons"][r], mb_toks[-1]], sem="eV")
        s2 = pr.add(
            "vector",
            lambda e, r=r: e.tensor_mul(xnp[r], xnp[r], Rb),
            deps=[s1], sem="eV")
        p5["xnp_tok"][r] = s2
        p5["xpb_cons"][b] = s1

    for fc0 in range(0, 6):
        emit_p5_xnp(fc0)

    def emit_p5_up(h):
        hb = h % 2
        if h < 2:
            lu = up_loads[h]
        else:
            lu = pr.add("gpsimd", lambda e, hb=hb, h=h: e.dma_start(out=upb[hb], in_=up_h[h]),
                        deps=[p5["up_cons"][hb]], sem="qUP%d" % hb, amt=16)
        gelu_toks = {}
        last_up = None
        for mb in range(8):
            for th in range(2):
                bank = p5["cnt"] % 8
                p5["cnt"] += 1

                def ugroup(e, hb=hb, h=h, mb=mb, th=th, bank=bank):
                    last = None
                    for kc in range(2):
                        last = e.matmul(
                            ps[bank],
                            upb[hb][:, kc * 1024 + mb * 128:kc * 1024 + mb * 128 + 128],
                            xnp[(2 * h + kc) % 8][:, th * 512:(th + 1) * 512],
                            start=(kc == 0), stop=(kc == 1))
                    return last

                mg = pr.add("tensor", ugroup,
                            deps=[lu, p5["xnp_tok"][(2 * h) % 8], p5["xnp_tok"][(2 * h + 1) % 8],
                                  p5["bank_ev"][bank], p5["t1g_cons"][hb]],
                            sem="eT")
                last_up = mg
                ge = pr.add(
                    "scalar",
                    lambda e, hb=hb, mb=mb, th=th, bank=bank: e.activation(
                        out=t1g[hb][mb][:, th * 512:(th + 1) * 512], in_=ps[bank], func=AF.Gelu),
                    deps=[mg, grp0_p4a_done], sem="eA")
                p5["bank_ev"][bank] = ge
                gelu_toks[(mb, th)] = ge
        p5["xnp_cons"][(2 * h) % 8] = last_up
        p5["xnp_cons"][(2 * h + 1) % 8] = last_up
        p5["up_cons"][hb] = last_up
        p5["gelu"][h] = gelu_toks

    def emit_p5_down(h):
        hb = h % 2
        if h < 2:
            ldn = pr.add("gpsimd", lambda e, hb=hb, h=h: e.dma_start(out=dnb[hb], in_=down_h[h]),
                         deps=[idx_toks[0], p3_last_mm], sem="qDN%d" % hb, amt=16)
        else:
            ldn = pr.add("gpsimd", lambda e, hb=hb, h=h: e.dma_start(out=dnb[hb], in_=down_h[h]),
                         deps=[p5["dn_cons"][hb]], sem="qDN%d" % hb, amt=16)
        gelu_toks = p5["gelu"].pop(h)
        gelu_max = max(gelu_toks.values(), key=lambda x: x[1])
        last_dn = None
        for db in range(2):
            gslot = db
            silu_toks = []
            for th in range(2):
                bank = p5["cnt"] % 8
                p5["cnt"] += 1

                def dgroup(e, hb=hb, db=db, th=th, bank=bank):
                    last = None
                    for mc in range(8):
                        last = e.matmul(
                            ps[bank],
                            dnb[hb][:, mc * 256 + db * 128:mc * 256 + db * 128 + 128],
                            t1g[hb][mc][:, th * 512:(th + 1) * 512],
                            start=(mc == 0), stop=(mc == 7))
                    return last

                mg = pr.add("tensor", dgroup,
                            deps=[ldn, p5["bank_ev"][bank], gelu_max, p5["gb_cons"][gslot]],
                            sem="eT")
                last_dn = mg
                si_ = pr.add(
                    "scalar",
                    lambda e, gslot=gslot, th=th, bank=bank: e.copy(
                        out=gbuf[gslot][:, th * 512:(th + 1) * 512], in_=ps[bank]),
                    deps=[mg], sem="eA")
                p5["bank_ev"][bank] = si_
                silu_toks.append(si_)
            sc = pr.add(
                "gpsimd",
                lambda e, gslot=gslot, h=h, db=db: e.indirect_dma_start(
                    out=g_hbm, out_offset=IndirectOffsetOnAxis(
                        perm_sb[:, 2 * h + db:2 * h + db + 1], 0),
                    in_=gbuf[gslot], in_offset=None),
                deps=silu_toks + [p0a, p5["scatters"][-1] if p5["scatters"] else None],
                sem="qSC", amt=16)
            p5["scatters"].append(sc)
            p5["gb_cons"][gslot] = sc
        p5["dn_cons"][hb] = last_dn
        p5["t1g_cons"][hb] = last_dn

    late_p4b = {0: 4, 2: 0, 4: 1, 6: 2, 8: 3}
    for h in range(16):
        if 2 * h + 6 < 32:
            emit_p5_xnp(2 * h + 6)
        if 2 * h + 7 < 32:
            emit_p5_xnp(2 * h + 7)
        if h < 4:
            emit_p4a_tile(h)
        emit_p5_up(h)
        if h >= 1:
            emit_p5_down(h - 1)
        if h in late_p4b:
            emit_p4b_tile(late_p4b[h], mbdep=mb_toks[-1] if late_p4b[h] == 0 else None)
    emit_p5_down(15)
    p5_dn_last = p5["dn_cons"][15 % 2]
    scatter_last = p5["scatters"][-1]

    g_tok = [AR[:, j * N:(j + 1) * N] for j in range(2)]
    sgb = [AR[:, 24576 + j * N:24576 + (j + 1) * N] for j in range(2)]
    g_cons = [None, None]
    sg_cons = [None, None]
    for t in range(NTILE):
        b = t % 2
        gx = pr.add(
            "sync",
            lambda e, b=b, t=t: e.dma_start(out=g_tok[b], in_=g_hbm[:, t * 128:(t + 1) * 128],
                                            transpose=True),
            deps=[g_cons[b], scatter_last], sem="qGX%d" % b, amt=16)
        sil = pr.add(
            "scalar",
            lambda e, b=b: e.activation(out=sgb[b], in_=g_tok[b], func=AF.Silu),
            deps=[gx, sg_cons[b]], sem="eA")
        g_cons[b] = sil
        sg = pr.add(
            "vector",
            lambda e, b=b, t=t: e.tensor_mul(sgb[b], S_sb[:, t * N:(t + 1) * N], sgb[b]),
            deps=[sil, fma_last[t]], sem="eV")
        ac1 = pr.add(
            "gpsimd",
            lambda e, b=b, t=t: e.dma_start(out=out_d[t * 128:(t + 1) * 128, 0:2048],
                                            in_=sgb[b][:, 0:2048], accum_op=ALU.add),
            deps=[sg, ("qOX", 16 * NTILE)], sem="qAC%d" % b, amt=16)
        ac2 = pr.add(
            "gpsimd",
            lambda e, b=b, t=t: e.dma_start(out=out_d[t * 128:(t + 1) * 128, 2048:4096],
                                            in_=sgb[b][:, 2048:4096], accum_op=ALU.add),
            deps=[sg], sem="qAC%d" % b, amt=16)
        sg_cons[b] = ac2

    if DEBUG:
        dbg = {}
        for nm, shape, dt in (("dbg_s", [128, NTILE * N], BF16), ("dbg_w", [128, 64], F32),
                              ("dbg_t8v", [128, 64], F32), ("dbg_idx", [128, 64], U32),
                              ("dbg_stats", [128, 16], F32), ("dbg_mb", [128, 1024], BF16),
                              ("dbg_rb", [128, 1024], BF16), ("dbg_g", [N, T], BF16)):
            dbg[nm] = nc.dram_tensor(nm, shape, dt, kind="ExternalOutput").ap()
        allv = [(s, v) for s, v in pr.cnt.items()]
        pr.add("sync", lambda e: e.dma_start(out=dbg["dbg_s"], in_=S_sb), deps=allv, sem="qS", amt=16)
        pr.add("sync", lambda e: e.dma_start(out=dbg["dbg_w"], in_=wsm), deps=allv, sem="qS", amt=16)
        pr.add("sync", lambda e: e.dma_start(out=dbg["dbg_t8v"], in_=t8v), deps=allv, sem="qS", amt=16)
        pr.add("sync", lambda e: e.dma_start(out=dbg["dbg_idx"], in_=idxw), deps=allv, sem="qS", amt=16)
        pr.add("sync", lambda e: e.dma_start(out=dbg["dbg_stats"], in_=stats), deps=allv, sem="qS", amt=16)
        pr.add("sync", lambda e: e.dma_start(out=dbg["dbg_mb"], in_=Mb), deps=allv, sem="qS", amt=16)
        pr.add("sync", lambda e: e.dma_start(out=dbg["dbg_rb"], in_=Rb), deps=allv, sem="qS", amt=16)
        pr.add("sync", lambda e: e.dma_start(out=dbg["dbg_g"], in_=g_hbm), deps=allv, sem="qS", amt=16)

    pr.add("sync", None, deps=[("qAC0", pr.cnt["qAC0"]), ("qAC1", pr.cnt["qAC1"]),
                               ("qOX", pr.cnt["qOX"])])

    with es:
        with nc.Block() as block:
            def emit(elist):
                def body(e):
                    for fn, waits, tok in elist:
                        for s in sorted(waits):
                            e.wait_ge(sems[s], waits[s])
                        if fn is None:
                            continue
                        ins = fn(e)
                        if tok is not None:
                            ins.then_inc(sems[tok[0]], 16 if tok[0].startswith("q") else 1)
                return body

            block.sync(emit(pr.items["sync"]))
            block.gpsimd(emit(pr.items["gpsimd"]))
            block.vector(emit(pr.items["vector"]))
            block.scalar(emit(pr.items["scalar"]))
            block.tensor(emit(pr.items["tensor"]))
    return nc


def kernel(L0_attn_x, L0_ffn_up, L0_ffn_down, L0_ffn_perm):
    global _NC_CACHE
    x = np.ascontiguousarray(np.asarray(L0_attn_x, np.float32).reshape(8192, N))
    up = np.asarray(L0_ffn_up, np.float32)
    down = np.asarray(L0_ffn_down, np.float32)
    perm = np.asarray(L0_ffn_perm).astype(np.int64)

    Fc_h, Fs_h, Cc_h, ident = _build_consts()
    up_h = np.ascontiguousarray(
        up.reshape(16, 2, 128, 1024).transpose(0, 2, 1, 3).reshape(16, 128, 2048))
    down_h = np.ascontiguousarray(
        down.reshape(16, 8, 128, 256).transpose(0, 2, 1, 3).reshape(16, 128, 2048))
    perm_t = np.ascontiguousarray(perm.reshape(32, 128).T).astype(np.uint32)
    rowb = ((np.arange(8)[None, :] * 128 + np.arange(128)[:, None]) * (2 * N)).astype(np.uint32)

    in_maps = []
    for c in range(NCORES):
        X = x[c * T:(c + 1) * T]
        Xt = np.ascontiguousarray(X.T)
        xf = np.zeros((NBIN, T), np.float32)
        xf[0:2049] = Xt[0:2049]
        xr = np.zeros((NBIN, T), np.float32)
        xr[0] = Xt[0]
        xr[1:2049] = Xt[4095:2047:-1]
        x_pfc = np.ascontiguousarray(Xt[perm])
        in_maps.append({
            "x_tok": X, "xf_in": xf, "xr_in": xr, "x_pf": x_pfc,
            "up_h": up_h, "down_h": down_h, "perm_t": perm_t, "rowb_in": rowb,
            "fc_in": Fc_h, "fs_in": Fs_h, "cc_in": Cc_h, "id_in": ident,
        })

    if _NC_CACHE is None:
        _NC_CACHE = build_nc()
    res = run_bass_kernel_spmd(_NC_CACHE, in_maps, core_ids=list(range(NCORES)))
    out = np.concatenate([np.asarray(r["out_d"], np.float32) for r in res.results], axis=0)
    return out.reshape(1, 8192, N)


# version 17
# speedup vs baseline: 1.0695x; 1.0544x over previous
import numpy as np
import ml_dtypes

try:
    import concourse.bass as bass
except ImportError:
    import sys
    sys.path.insert(0, "/opt/trn_rl_repo")
    import concourse.bass as bass
import concourse.mybir as mybir
from concourse.bass_utils import run_bass_kernel_spmd
from concourse.bass_core_rust import IndirectOffsetOnAxis
from contextlib import ExitStack

F32 = mybir.dt.float32
BF16 = mybir.dt.bfloat16
U32 = mybir.dt.uint32
AF = mybir.ActivationFunctionType
ALU = mybir.AluOpType
AX = mybir.AxisListType.X
BF = ml_dtypes.bfloat16

NCORES = 8
T = 1024
NTILE = 8
N = 4096
KB = 17
NBIN = 2176
TAUB = 8

ENGS = ("sync", "gpsimd", "vector", "scalar", "tensor")


class Prog:

    def __init__(self):
        self.items = {e: [] for e in ENGS}
        self.cnt = {}

    def add(self, eng, fn, deps=(), sem=None, amt=1):
        waits = {}
        for d in deps:
            if d is None:
                continue
            s, v = d
            if v > 0:
                waits[s] = max(waits.get(s, 0), v)
        tok = None
        if sem is not None:
            self.cnt[sem] = self.cnt.get(sem, 0) + amt
            tok = (sem, self.cnt[sem])
        self.items[eng].append((fn, waits, tok))
        return tok


_CONSTS = None


def _build_consts():
    global _CONSTS
    if _CONSTS is not None:
        return _CONSTS
    kk = np.arange(2049, dtype=np.float64)
    ang = 2.0 * np.pi * np.outer(kk, kk) / N
    Fc = np.cos(ang)
    Fc[0] *= 0.5
    Fc[2048] *= 0.5
    Fs = np.sin(ang)
    Fs[0] = 0.0
    Fs[2048] = 0.0
    Fcp = np.zeros((NBIN, NBIN), np.float64)
    Fcp[:2049, :2049] = Fc
    Fsp = np.zeros((NBIN, NBIN), np.float64)
    Fsp[:2049, :2049] = Fs
    cm = np.full(NBIN, -1, np.int64)
    cm[0:1025] = np.arange(0, 2049, 2)
    cm[1152:2176] = np.arange(1, 2048, 2)
    Fcp = Fcp[:, :2049][:, np.clip(cm, 0, None)] * (cm >= 0)[None, :]
    Fsp = Fsp[:, :2049][:, np.clip(cm, 0, None)] * (cm >= 0)[None, :]
    tau = np.arange(2048, dtype=np.float64)
    coef = np.where(cm == 0, 0.0, np.where(cm == 2048, 1.0 / N, 2.0 / N))
    coef = coef * (cm >= 0)
    Ccp = coef[:, None] * np.cos(2.0 * np.pi * np.outer(np.clip(cm, 0, None), tau) / N)
    Fc_h = np.ascontiguousarray(
        Fcp.reshape(KB, 128, KB, 128).transpose(2, 1, 0, 3).reshape(KB, 128, NBIN)
    ).astype(BF)
    Fs_h = np.ascontiguousarray(
        Fsp.reshape(KB, 128, KB, 128).transpose(2, 1, 0, 3).reshape(KB, 128, NBIN)
    ).astype(BF)
    Cc_h = np.ascontiguousarray(
        Ccp.reshape(KB, 128, 4, 512).transpose(2, 1, 0, 3).reshape(4, 128, KB * 512)
    ).astype(BF)
    ident = np.eye(128, dtype=np.float32)
    _CONSTS = (Fc_h, Fs_h, Cc_h, ident)
    return _CONSTS


_NC_CACHE = None
DEBUG = False


def build_nc():
    nc = bass.Bass("TRN2", target_bir_lowering=False, dynamic_dma_scratch_size=16384)

    x_tok = nc.dram_tensor("x_tok", [T, N], F32, kind="ExternalInput").ap()
    xf_in = nc.dram_tensor("xf_in", [NBIN, T], F32, kind="ExternalInput").ap()
    xr_in = nc.dram_tensor("xr_in", [NBIN, T], F32, kind="ExternalInput").ap()
    x_pf = nc.dram_tensor("x_pf", [N, T], F32, kind="ExternalInput").ap()
    up_h = nc.dram_tensor("up_h", [16, 128, 2048], F32, kind="ExternalInput").ap()
    down_h = nc.dram_tensor("down_h", [16, 128, 2048], F32, kind="ExternalInput").ap()
    perm_t = nc.dram_tensor("perm_t", [128, 32], U32, kind="ExternalInput").ap()
    rowb_in = nc.dram_tensor("rowb_in", [128, 8], U32, kind="ExternalInput").ap()
    fc_in = nc.dram_tensor("fc_in", [KB, 128, NBIN], BF16, kind="ExternalInput").ap()
    fs_in = nc.dram_tensor("fs_in", [KB, 128, NBIN], BF16, kind="ExternalInput").ap()
    cc_in = nc.dram_tensor("cc_in", [4, 128, KB * 512], BF16, kind="ExternalInput").ap()
    id_in = nc.dram_tensor("id_in", [128, 128], F32, kind="ExternalInput").ap()

    out_d = nc.dram_tensor("out_d", [T, N], F32, kind="ExternalOutput").ap()

    xn_dbl = nc.dram_tensor("xn_dbl", [T * 2 * N], BF16, kind="Internal").ap()
    xn_rows = xn_dbl.rearrange("(t f) -> t f", f=2 * N)
    xn_flat = xn_dbl.rearrange("(a b) -> a b", b=1)
    g_hbm = nc.dram_tensor("g_hbm", [N, T], BF16, kind="Internal").ap()

    es = ExitStack()
    sb = lambda name, shape, dt: es.enter_context(nc.sbuf_tensor(name, shape, dt)).ap()

    AR = sb("arena", [128, 51456], BF16)
    ARF = AR.bitcast(F32)
    P_sb = sb("p_sb", [128, KB * 1024], BF16)
    P_f32 = P_sb.bitcast(F32)
    S_sb = sb("s_sb", [128, NTILE * N], BF16)
    S_f32 = S_sb.bitcast(F32)
    stats = sb("stats", [128, 16], F32)
    msum8 = sb("msum8", [128, 8], F32)
    qcol8 = sb("qcol8", [128, 8], F32)
    tmp8b = sb("tmp8b", [128, 8], F32)
    tmp8c = sb("tmp8c", [128, 8], F32)
    tmp8d = sb("tmp8d", [128, 8], F32)
    rsqc = sb("rsqc", [128, 8], F32)
    nmrc = sb("nmrc", [128, 8], F32)
    t8v = sb("t8v", [128, 64], F32)
    t8i = sb("t8i", [128, 64], U32)
    idxw = sb("idxw", [128, 64], U32)
    wsm = sb("wsm", [128, 64], F32)
    we8 = sb("we8", [128, 8], F32)
    ex8 = sb("ex8", [128, 8], F32)
    sm1 = sb("sm1", [128, 1], F32)
    sm2 = sb("sm2", [128, 1], F32)
    rowb = sb("rowb", [128, 8], U32)
    perm_sb = sb("perm_sb", [128, 32], U32)
    ident_sb = sb("ident_sb", [128, 128], F32)
    ones1 = sb("ones1", [1, 128], F32)
    statsT = sb("statsT", [16, 128], F32)
    cvals = sb("cvals", [128, 2], F32)
    mrow = S_f32[0:1, 0:1024]
    rrow = S_f32[0:1, 1024:2048]
    ptmp = S_sb[:, 29696:30720]

    ps = [es.enter_context(nc.psum_tensor(f"ps{i}", [128, 512], F32)).ap() for i in range(8)]

    sems = {}
    for name in ("qS", "qCC0", "qCC1", "qXT", "qXN", "qROW", "qGX0", "qGX1",
                 "qLX0", "qLX1", "qST0", "qST1", "qOX", "qAC0", "qAC1",
                 "qXF0", "qXF1", "qXR0", "qXR1", "qF0", "qF1",
                 "qUP0", "qUP1", "qDN0", "qDN1", "qXP0", "qXP1",
                 "qGA0", "qGA1", "qGA2", "qGA3", "qSC",
                 "eV", "eA", "eT"):
        sems[name] = es.enter_context(nc.semaphore(name))

    ccbuf = [S_sb[:, 0:KB * 512], S_sb[:, KB * 512:2 * KB * 512]]
    corr = [AR[:, t * N:(t + 1) * N] for t in range(NTILE)]

    pr = Prog()

    p0a = pr.add("sync", lambda e: e.dma_start(out=perm_sb, in_=perm_t), sem="qS", amt=16)
    p0b = pr.add("sync", lambda e: e.dma_start(out=rowb, in_=rowb_in), deps=[p0a], sem="qS", amt=16)
    p0c = pr.add("sync", lambda e: e.dma_start(out=ident_sb, in_=id_in), deps=[p0b], sem="qS", amt=16)
    c_one = pr.add("vector", lambda e: e.memset(ones1, 1.0), sem="eV")
    c_eps = pr.add("vector", lambda e: e.memset(cvals[:, 0:1], 1e-5), sem="eV")
    c_half = pr.add("vector", lambda e: e.memset(cvals[:, 1:2], -0.5), sem="eV")
    cc_load = [pr.add("sync", lambda e, tb=tb: e.dma_start(out=ccbuf[tb], in_=cc_in[tb]),
                      sem="qCC%d" % tb, amt=16) for tb in range(2)]

    xF = ARF[:, 21632:25728]
    dumpb = S_sb[:, 17408:21504]
    xnb = [S_sb[:, 21504 + j * N:21504 + (j + 1) * N] for j in range(2)]
    eP = AR[:, 0:KB * 1024]
    oP = AR[:, KB * 1024:2 * KB * 1024]
    xfb = [AR[:, 34816 + j * 1024:34816 + (j + 1) * 1024] for j in range(2)]
    xrb = [AR[:, 36864 + j * 1024:36864 + (j + 1) * 1024] for j in range(2)]
    fslot = [AR[:, 38912 + j * NBIN:38912 + (j + 1) * NBIN] for j in range(2)]

    xn_write_toks = []
    rstd_toks = [None] * NTILE
    rsq_toks = [None] * NTILE
    xn_prev = None
    sq_prev = None
    xF_free = None
    eo_last = [None, None]
    eo_final = None

    def emit_p1_tile(i):
        nonlocal xn_prev, sq_prev, xF_free
        ld = pr.add(
            "sync",
            lambda e, i=i: e.dma_start(out=xF, in_=x_tok[i * 128:(i + 1) * 128, :]),
            deps=[xF_free, getattr(pr, "last_ox", None)], sem="qXT", amt=16)
        pr.last_ox = pr.add(
            "sync",
            lambda e, i=i: e.dma_start(out=out_d[i * 128:(i + 1) * 128, :], in_=xF),
            deps=[ld], sem="qOX", amt=16)
        rs = pr.add(
            "vector",
            lambda e, i=i: e.reduce_sum(msum8[:, i:i + 1], xF, axis=AX),
            deps=[ld], sem="eV")
        sq = pr.add(
            "vector",
            lambda e, i=i: e.scalar_tensor_tensor(
                dumpb, xF, 1.0, xF, ALU.mult, ALU.mult, accum_out=qcol8[:, i:i + 1]),
            deps=[ld, sq_prev], sem="eV")
        sq_prev = sq
        m = pr.add(
            "vector",
            lambda e, i=i: e.tensor_scalar(stats[:, i:i + 1], msum8[:, i:i + 1], 1.0 / N, None, ALU.mult),
            deps=[rs], sem="eV")
        t1 = pr.add(
            "vector",
            lambda e, i=i: e.tensor_mul(tmp8b[:, i:i + 1], stats[:, i:i + 1], stats[:, i:i + 1]),
            deps=[m], sem="eV")
        var = pr.add(
            "vector",
            lambda e, i=i: e.tensor_scalar(
                tmp8c[:, i:i + 1], qcol8[:, i:i + 1], 1.0 / N, tmp8b[:, i:i + 1],
                ALU.mult, ALU.subtract),
            deps=[sq, t1], sem="eV")
        lnv = pr.add(
            "scalar",
            lambda e, i=i: e.activation(out=tmp8d[:, i:i + 1], in_=tmp8c[:, i:i + 1],
                                        func=AF.Ln, bias=cvals[:, 0:1]),
            deps=[var, c_eps], sem="eA")
        rstd = pr.add(
            "scalar",
            lambda e, i=i: e.activation(out=stats[:, 8 + i:9 + i], in_=tmp8d[:, i:i + 1],
                                        func=AF.Exp, scale=cvals[:, 1:2]),
            deps=[lnv, c_half], sem="eA")
        rstd_toks[i] = rstd
        rsq_toks[i] = pr.add(
            "vector",
            lambda e, i=i: e.tensor_mul(rsqc[:, i:i + 1], stats[:, 8 + i:9 + i], stats[:, 8 + i:9 + i]),
            deps=[rstd], sem="eV")
        nmr = pr.add(
            "vector",
            lambda e, i=i: e.scalar_tensor_tensor(
                nmrc[:, i:i + 1], stats[:, i:i + 1], -1.0, stats[:, 8 + i:9 + i],
                ALU.mult, ALU.mult),
            deps=[m, rstd], sem="eV")
        b = i % 2
        xn = pr.add(
            "scalar",
            lambda e, b=b, i=i: e.activation(out=xnb[b], in_=xF, func=AF.Identity,
                                             bias=nmrc[:, i:i + 1], scale=stats[:, 8 + i:9 + i]),
            deps=[rstd, nmr, ld, xn_prev], sem="eA")
        xF_free = xn
        w1 = pr.add(
            "sync",
            lambda e, b=b, i=i: e.dma_start(out=xn_rows[i * 128:(i + 1) * 128, 0:N], in_=xnb[b]),
            deps=[xn], sem="qXN", amt=16)
        w2 = pr.add(
            "sync",
            lambda e, b=b, i=i: e.dma_start(out=xn_rows[i * 128:(i + 1) * 128, N:2 * N], in_=xnb[b]),
            deps=[xn], sem="qXN", amt=16)
        xn_write_toks.append(w2)
        xn_prev = w2

    def emit_p2_chunk(fc):
        nonlocal eo_final
        b = fc % 2
        lf = pr.add(
            "gpsimd",
            lambda e, b=b, fc=fc: e.dma_start(out=xfb[b], in_=xf_in[fc * 128:(fc + 1) * 128, :]),
            deps=[eo_last[b]], sem="qXF%d" % b, amt=16)
        lr = pr.add(
            "gpsimd",
            lambda e, b=b, fc=fc: e.dma_start(out=xrb[b], in_=xr_in[fc * 128:(fc + 1) * 128, :]),
            deps=[eo_last[b]], sem="qXR%d" % b, amt=16)
        te = pr.add(
            "vector",
            lambda e, b=b, fc=fc: e.tensor_add(eP[:, fc * 1024:(fc + 1) * 1024], xfb[b], xrb[b]),
            deps=[lf, lr], sem="eV")
        to = pr.add(
            "vector",
            lambda e, b=b, fc=fc: e.scalar_tensor_tensor(
                oP[:, fc * 1024:(fc + 1) * 1024], xfb[b], 2.0,
                eP[:, fc * 1024:(fc + 1) * 1024], ALU.mult, ALU.subtract),
            deps=[te], sem="eV")
        eo_last[b] = to
        eo_final = to

    for step in range(KB):
        emit_p2_chunk(step)

    f_cons_tok = [None, None]
    bank_mm = [None] * 8
    bank_ev = [None] * 8
    ptmp_war = [None, None]
    p_last = None
    si = 0
    for bb in range(KB):
        for pi, src in ((0, fc_in), (1, fs_in)):
            slot = si % 2
            si += 1
            fl = pr.add(
                "gpsimd",
                lambda e, slot=slot, src=src, bb=bb: e.dma_start(out=fslot[slot], in_=src[bb]),
                deps=[f_cons_tok[slot]], sem="qF%d" % slot, amt=16)
            p_c_tok = None
            for th in range(2):
                bank = ((bb * 2 + th) * 2 + pi) % 8
                eo = eP if pi == 0 else oP

                def mmgroup(e, slot=slot, bank=bank, eo=eo, th=th):
                    last = None
                    for fc in range(KB):
                        last = e.matmul(
                            ps[bank],
                            fslot[slot][:, fc * 128:(fc + 1) * 128],
                            eo[:, fc * 1024 + th * 512:fc * 1024 + th * 512 + 512],
                            start=(fc == 0), stop=(fc == KB - 1))
                    return last

                mg = pr.add("tensor", mmgroup,
                            deps=[fl, eo_final, bank_ev[bank]], sem="eT")
                bank_mm[bank] = mg
                f_cons_tok[slot] = mg
                dst = P_sb[:, bb * 1024 + th * 512: bb * 1024 + th * 512 + 512]
                if pi == 0:
                    ev = pr.add(
                        "scalar",
                        lambda e, bank=bank, dst=dst: e.activation(out=dst, in_=ps[bank], func=AF.Square),
                        deps=[mg], sem="eA")
                    bank_ev[bank] = ev
                    p_c_tok = ev
                else:
                    ev = pr.add(
                        "scalar",
                        lambda e, bank=bank, th=th: e.activation(
                            out=ptmp[:, th * 512:(th + 1) * 512], in_=ps[bank], func=AF.Square),
                        deps=[mg, ptmp_war[th]], sem="eA")
                    bank_ev[bank] = ev
                    pa = pr.add(
                        "vector",
                        lambda e, dst=dst, th=th: e.tensor_add(
                            dst, dst, ptmp[:, th * 512:(th + 1) * 512]),
                        deps=[ev, p_c_tok], sem="eV")
                    ptmp_war[th] = pa
                    p_last = pa
        if bb < NTILE:
            emit_p1_tile(bb)

    cc_cons = [None, None]
    tile_ev = [None] * NTILE
    ctx = {"p3_last": None, "grp0_p4a": None}

    p4a = {"prev_ex": None}
    find_toks = [None] * NTILE
    idx_toks = [None] * NTILE
    w_toks = [None] * NTILE

    def emit_p4a_tile(t):
        c = corr[t]
        mx = pr.add("vector",
                    lambda e, t=t, c=c: e.max(out=t8v[:, t * 8:(t + 1) * 8], in_=c),
                    deps=[tile_ev[t], rsq_toks[t]], sem="eV")
        fi = pr.add("vector",
                    lambda e, t=t, c=c: e.max_index(out=t8i[:, t * 8:(t + 1) * 8],
                                                    in_max=t8v[:, t * 8:(t + 1) * 8], in_values=c),
                    deps=[mx], sem="eV")
        find_toks[t] = fi
        wv = pr.add("vector",
                    lambda e, t=t: e.tensor_scalar(wsm[:, t * 8:(t + 1) * 8], t8v[:, t * 8:(t + 1) * 8],
                                                   rsqc[:, t:t + 1], None, ALU.mult),
                    deps=[mx], sem="eV")
        sub = pr.add("vector",
                     lambda e, t=t: e.tensor_scalar(we8, wsm[:, t * 8:(t + 1) * 8],
                                                    wsm[:, t * 8:t * 8 + 1], None, ALU.subtract),
                     deps=[wv, p4a["prev_ex"]], sem="eV")
        ex = pr.add("scalar",
                    lambda e: e.activation(out=ex8, in_=we8, func=AF.Exp),
                    deps=[sub], sem="eA")
        p4a["prev_ex"] = ex
        ssum = pr.add("vector", lambda e: e.reduce_sum(sm1, ex8, axis=AX), deps=[ex], sem="eV")
        rsum = pr.add("vector", lambda e: e.reciprocal(sm2, sm1), deps=[ssum], sem="eV")
        w_toks[t] = pr.add("vector",
                           lambda e, t=t: e.tensor_scalar(wsm[:, t * 8:(t + 1) * 8], ex8,
                                                          sm2[:, 0:1], None, ALU.mult),
                           deps=[rsum, sub], sem="eV")
        idx_toks[t] = pr.add(
            "vector",
            lambda e, t=t: e.tensor_tensor(idxw[:, t * 8:(t + 1) * 8], t8i[:, t * 8:(t + 1) * 8],
                                           rowb[:, t:t + 1].to_broadcast([128, 8]), ALU.add),
            deps=[find_toks[t], p0b], sem="eV")
        return idx_toks[t]

    roll = [AR[:, 32768 + j * N:32768 + (j + 1) * N] for j in range(4)]
    roll_cons = [None, None, None, None]
    fma_last = [None] * NTILE

    p4b_state = {}

    def emit_p4b_tile(t, mbdep=None, ks=None):
        fma_prev = p4b_state.get(t)
        if ks is None:
            ks = range(8)
        for k in ks:
            rb = k % 4
            ga = pr.add(
                "gpsimd",
                lambda e, t=t, k=k, rb=rb: e.indirect_dma_start(
                    out=roll[rb], out_offset=None, in_=xn_flat,
                    in_offset=IndirectOffsetOnAxis(idxw[:, t * 8 + k:t * 8 + k + 1], 0)),
                deps=[idx_toks[t], roll_cons[rb], xn_write_toks[-1]],
                sem="qGA%d" % rb, amt=16)
            st = S_sb[:, t * N:(t + 1) * N]
            if k == 0:
                extra = [ctx["p3_last"]] if t <= 4 else []
                if mbdep is not None:
                    extra.append(mbdep)
                fm = pr.add(
                    "vector",
                    lambda e, t=t, rb=rb, st=st: e.tensor_scalar(
                        st, roll[rb], wsm[:, t * 8:t * 8 + 1], None, ALU.mult),
                    deps=[ga, w_toks[t]] + extra, sem="eV")
            else:
                sc_ = pr.add(
                    "vector",
                    lambda e, t=t, k=k, rb=rb: e.tensor_scalar(
                        roll[rb], roll[rb], wsm[:, t * 8 + k:t * 8 + k + 1], None, ALU.mult),
                    deps=[ga, w_toks[t]], sem="eV")
                fm = pr.add(
                    "vector",
                    lambda e, rb=rb, st=st: e.tensor_add(st, st, roll[rb]),
                    deps=[sc_, fma_prev], sem="eV")
            fma_prev = fm
            roll_cons[rb] = fm
        p4b_state[t] = fma_prev
        fma_last[t] = fma_prev

    otmp = [S_sb[:, 17408 + j * 512:17408 + (j + 1) * 512] for j in range(2)]
    otmp_war = [None, None]
    bank_free = list(bank_ev)
    L = 0
    p3_mg = None
    for grp, tiles in ((0, (4, 5, 6, 7)), (1, (0, 1, 2, 3))):
        for tb in range(4):
            b = L % 2
            if L >= 2:
                cc_load[b] = pr.add(
                    "sync",
                    lambda e, b=b, tb=tb: e.dma_start(out=ccbuf[b], in_=cc_in[tb]),
                    deps=[cc_cons[b]], sem="qCC%d" % b, amt=16)
            L += 1
            for i, t in enumerate(tiles):
                bA, bB = 2 * i, 2 * i + 1

                def egroup(e, b=b, t=t, bA=bA):
                    last = None
                    for kc in range(9):
                        last = e.matmul(
                            ps[bA],
                            P_sb[:, kc * 1024 + t * 128:kc * 1024 + t * 128 + 128],
                            ccbuf[b][:, kc * 512:(kc + 1) * 512],
                            start=(kc == 0), stop=(kc == 8))
                    return last

                def ogroup(e, b=b, t=t, bB=bB):
                    last = None
                    for kc in range(9, KB):
                        last = e.matmul(
                            ps[bB],
                            P_sb[:, kc * 1024 + t * 128:kc * 1024 + t * 128 + 128],
                            ccbuf[b][:, kc * 512:(kc + 1) * 512],
                            start=(kc == 9), stop=(kc == KB - 1))
                    return last

                mge = pr.add("tensor", egroup,
                             deps=[cc_load[b], p_last, bank_free[bA]], sem="eT")
                mgo = pr.add("tensor", ogroup,
                             deps=[cc_load[b], p_last, bank_free[bB]], sem="eT")
                p3_mg = mgo
                cc_cons[b] = mgo
                ot = pr.add(
                    "scalar",
                    lambda e, bB=bB, ob=tb % 2: e.copy(out=otmp[ob], in_=ps[bB]),
                    deps=[mgo, otmp_war[tb % 2]], sem="eA")
                bank_free[bB] = ot
                lo = pr.add(
                    "vector",
                    lambda e, t=t, tb=tb, bA=bA, ob=tb % 2: e.tensor_add(
                        corr[t][:, tb * 512:(tb + 1) * 512], ps[bA], otmp[ob]),
                    deps=[mge, ot], sem="eV")
                hi = pr.add(
                    "vector",
                    lambda e, t=t, tb=tb, bA=bA, ob=tb % 2: e.tensor_sub(
                        corr[t][:, 2048 + tb * 512:2048 + (tb + 1) * 512], ps[bA], otmp[ob]),
                    deps=[lo], sem="eV")
                bank_free[bA] = hi
                otmp_war[tb % 2] = hi
                tile_ev[t] = hi
            if grp == 1:
                for (tt, kp) in {0: [(5, 0)], 1: [(5, 1), (5, 2)],
                                 2: [(5, 3), (6, 0), (6, 1)],
                                 3: [(6, 2), (6, 3), (7, 0)]}[tb]:
                    emit_p4b_tile(tt, ks=range(2 * kp, 2 * kp + 2))
        if grp == 0:
            for t in (5, 7, 4, 6):
                ctx["grp0_p4a"] = emit_p4a_tile(t)
    p3_last_mm = p3_mg
    ctx["p3_last"] = p3_last_mm
    grp0_p4a_done = ctx["grp0_p4a"]

    xnp = [P_sb[:, j * 1024:(j + 1) * 1024] for j in range(8)]
    xpb = [P_sb[:, 8192 + j * 1024:8192 + (j + 1) * 1024] for j in range(2)]
    upb = [P_sb[:, 10240 + j * 2048:10240 + (j + 1) * 2048] for j in range(2)]
    dnb = [AR[:, j * 2048:(j + 1) * 2048] for j in range(2)]
    Mb = P_sb[:, 14336:15360]
    Rb = P_sb[:, 15360:16384]
    t1g = [[AR[:, 16384 + hh * 8192 + mb * 1024:16384 + hh * 8192 + (mb + 1) * 1024]
            for mb in range(8)] for hh in range(2)]
    gbuf = [AR[:, 49152 + j * 1024:49152 + (j + 1) * 1024] for j in range(2)]

    up_loads = {}
    xpb_loads = {}
    for h in range(2):
        up_loads[h] = pr.add("gpsimd", lambda e, h=h: e.dma_start(out=upb[h % 2], in_=up_h[h]),
                             deps=[p3_last_mm], sem="qUP%d" % (h % 2), amt=16)
    for fc in range(2):
        xpb_loads[fc] = pr.add(
            "gpsimd",
            lambda e, fc=fc: e.dma_start(out=xpb[fc % 2], in_=x_pf[fc * 128:(fc + 1) * 128, :]),
            deps=[p3_last_mm], sem="qXP%d" % (fc % 2), amt=16)

    tr = pr.add("tensor",
                lambda e: e.transpose(ps[0][0:16, 0:128], stats[:, 0:16], ident_sb),
                deps=[bank_free[0], rstd_toks[-1], p0c], sem="eT")
    cpT = pr.add("vector",
                 lambda e: e.tensor_copy(statsT[0:16, :], ps[0][0:16, 0:128]),
                 deps=[tr], sem="eV")
    row_toks = []
    for j in range(8):
        row_toks.append(pr.add(
            "sync",
            lambda e, j=j: e.dma_start(out=mrow[0:1, j * 128:(j + 1) * 128], in_=statsT[j:j + 1, :]),
            deps=[cpT, p3_last_mm], sem="qROW", amt=16))
        row_toks.append(pr.add(
            "sync",
            lambda e, j=j: e.dma_start(out=rrow[0:1, j * 128:(j + 1) * 128], in_=statsT[8 + j:9 + j, :]),
            deps=[cpT, p3_last_mm], sem="qROW", amt=16))
    mb_toks = []
    for half in range(2):
        for ri, (rowap, dstap) in enumerate(((mrow, Mb), (rrow, Rb))):
            bank = 1 + half * 2 + ri
            mm = pr.add(
                "tensor",
                lambda e, bank=bank, rowap=rowap, half=half: e.matmul(
                    ps[bank], ones1[0:1, :], rowap[0:1, half * 512:(half + 1) * 512],
                    start=True, stop=True),
                deps=row_toks + [c_one, bank_free[bank]], sem="eT")
            mb_toks.append(pr.add(
                "scalar",
                lambda e, bank=bank, dstap=dstap, half=half: e.copy(
                    out=dstap[:, half * 512:(half + 1) * 512], in_=ps[bank]),
                deps=[mm], sem="eA"))

    p5 = {"up_cons": [None, None], "dn_cons": [None, None],
          "t1g_cons": [None, None], "gb_cons": [None, None],
          "bank_ev": [cpT, mb_toks[0], mb_toks[1], mb_toks[2], mb_toks[3],
                      bank_free[5], bank_free[6], bank_free[7]],
          "cnt": 0, "scatters": [], "xnp_tok": [None] * 8, "xnp_cons": [None] * 8,
          "xpb_cons": [None, None], "gelu": {}, "pend": None}

    def emit_p5_xnp(fc):
        b = fc % 2
        r = fc % 8
        if fc < 2:
            lx = xpb_loads[fc]
        else:
            lx = pr.add(
                "gpsimd",
                lambda e, b=b, fc=fc: e.dma_start(out=xpb[b], in_=x_pf[fc * 128:(fc + 1) * 128, :]),
                deps=[p5["xpb_cons"][b]], sem="qXP%d" % b, amt=16)
        s1 = pr.add(
            "vector",
            lambda e, b=b, r=r: e.tensor_sub(xnp[r], xpb[b], Mb),
            deps=[lx, p5["xnp_cons"][r], mb_toks[-1]], sem="eV")
        s2 = pr.add(
            "vector",
            lambda e, r=r: e.tensor_mul(xnp[r], xnp[r], Rb),
            deps=[s1], sem="eV")
        p5["xnp_tok"][r] = s2
        p5["xpb_cons"][b] = s1

    for fc0 in range(0, 8):
        emit_p5_xnp(fc0)

    def emit_p5_up(h):
        hb = h % 2
        if h < 2:
            lu = up_loads[h]
        else:
            lu = pr.add("gpsimd", lambda e, hb=hb, h=h: e.dma_start(out=upb[hb], in_=up_h[h]),
                        deps=[p5["up_cons"][hb]], sem="qUP%d" % hb, amt=16)
        gelu_toks = {}
        last_up = None
        for mb in range(8):
            for th in range(2):
                bank = p5["cnt"] % 8
                p5["cnt"] += 1

                def ugroup(e, hb=hb, h=h, mb=mb, th=th, bank=bank):
                    last = None
                    for kc in range(2):
                        last = e.matmul(
                            ps[bank],
                            upb[hb][:, kc * 1024 + mb * 128:kc * 1024 + mb * 128 + 128],
                            xnp[(2 * h + kc) % 8][:, th * 512:(th + 1) * 512],
                            start=(kc == 0), stop=(kc == 1))
                    return last

                mg = pr.add("tensor", ugroup,
                            deps=[lu, p5["xnp_tok"][(2 * h) % 8], p5["xnp_tok"][(2 * h + 1) % 8],
                                  p5["bank_ev"][bank], p5["t1g_cons"][hb]],
                            sem="eT")
                last_up = mg
                ge = pr.add(
                    "scalar",
                    lambda e, hb=hb, mb=mb, th=th, bank=bank: e.activation(
                        out=t1g[hb][mb][:, th * 512:(th + 1) * 512], in_=ps[bank], func=AF.Gelu),
                    deps=[mg, grp0_p4a_done], sem="eA")
                p5["bank_ev"][bank] = ge
                gelu_toks[(mb, th)] = ge
        p5["xnp_cons"][(2 * h) % 8] = last_up
        p5["xnp_cons"][(2 * h + 1) % 8] = last_up
        p5["up_cons"][hb] = last_up
        p5["gelu"][h] = gelu_toks

    def emit_p5_down(h):
        hb = h % 2
        if h < 2:
            ldn = pr.add("gpsimd", lambda e, hb=hb, h=h: e.dma_start(out=dnb[hb], in_=down_h[h]),
                         deps=[idx_toks[0], p3_last_mm], sem="qDN%d" % hb, amt=16)
        else:
            ldn = pr.add("gpsimd", lambda e, hb=hb, h=h: e.dma_start(out=dnb[hb], in_=down_h[h]),
                         deps=[p5["dn_cons"][hb]], sem="qDN%d" % hb, amt=16)
        gelu_toks = p5["gelu"].pop(h)
        gelu_max = max(gelu_toks.values(), key=lambda x: x[1])
        last_dn = None
        for db in range(2):
            gslot = db
            silu_toks = []
            for th in range(2):
                bank = p5["cnt"] % 8
                p5["cnt"] += 1

                def dgroup(e, hb=hb, db=db, th=th, bank=bank):
                    last = None
                    for mc in range(8):
                        last = e.matmul(
                            ps[bank],
                            dnb[hb][:, mc * 256 + db * 128:mc * 256 + db * 128 + 128],
                            t1g[hb][mc][:, th * 512:(th + 1) * 512],
                            start=(mc == 0), stop=(mc == 7))
                    return last

                mg = pr.add("tensor", dgroup,
                            deps=[ldn, p5["bank_ev"][bank], gelu_max, p5["gb_cons"][gslot]],
                            sem="eT")
                last_dn = mg
                si_ = pr.add(
                    "scalar",
                    lambda e, gslot=gslot, th=th, bank=bank: e.copy(
                        out=gbuf[gslot][:, th * 512:(th + 1) * 512], in_=ps[bank]),
                    deps=[mg], sem="eA")
                p5["bank_ev"][bank] = si_
                silu_toks.append(si_)
            sc = pr.add(
                "gpsimd",
                lambda e, gslot=gslot, h=h, db=db: e.indirect_dma_start(
                    out=g_hbm, out_offset=IndirectOffsetOnAxis(
                        perm_sb[:, 2 * h + db:2 * h + db + 1], 0),
                    in_=gbuf[gslot], in_offset=None),
                deps=silu_toks + [p0a, p5["scatters"][-1] if p5["scatters"] else None],
                sem="qSC", amt=16)
            p5["scatters"].append(sc)
            p5["gb_cons"][gslot] = sc
        p5["dn_cons"][hb] = last_dn
        p5["t1g_cons"][hb] = last_dn

    for t in (0, 1, 2, 3):
        emit_p4a_tile(t)
    late_q = [(7, 1), (7, 2), (7, 3)]
    for tt in (4, 0, 1, 2, 3):
        late_q += [(tt, 0), (tt, 1), (tt, 2), (tt, 3)]
    qi = [0]

    def emit_quarter():
        if qi[0] < len(late_q):
            tt, kp = late_q[qi[0]]
            qi[0] += 1
            emit_p4b_tile(tt, mbdep=mb_toks[-1] if (tt == 0 and kp == 0) else None,
                          ks=range(2 * kp, 2 * kp + 2))

    for h in range(16):
        emit_quarter()
        emit_p5_up(h)
        if 2 * h + 8 < 32:
            emit_p5_xnp(2 * h + 8)
        emit_quarter()
        if 2 * h + 9 < 32:
            emit_p5_xnp(2 * h + 9)
        if h >= 1:
            emit_p5_down(h - 1)
    emit_p5_down(15)
    p5_dn_last = p5["dn_cons"][15 % 2]
    scatter_last = p5["scatters"][-1]

    g_tok = [AR[:, j * N:(j + 1) * N] for j in range(2)]
    sgb = [AR[:, 24576 + j * N:24576 + (j + 1) * N] for j in range(2)]
    g_cons = [None, None]
    sg_cons = [None, None]
    for t in range(NTILE):
        b = t % 2
        gx = pr.add(
            "sync",
            lambda e, b=b, t=t: e.dma_start(out=g_tok[b], in_=g_hbm[:, t * 128:(t + 1) * 128],
                                            transpose=True),
            deps=[g_cons[b], scatter_last], sem="qGX%d" % b, amt=16)
        sil = pr.add(
            "scalar",
            lambda e, b=b: e.activation(out=sgb[b], in_=g_tok[b], func=AF.Silu),
            deps=[gx, sg_cons[b]], sem="eA")
        g_cons[b] = sil
        sg = pr.add(
            "vector",
            lambda e, b=b, t=t: e.tensor_mul(sgb[b], S_sb[:, t * N:(t + 1) * N], sgb[b]),
            deps=[sil, fma_last[t]], sem="eV")
        ac1 = pr.add(
            "gpsimd",
            lambda e, b=b, t=t: e.dma_start(out=out_d[t * 128:(t + 1) * 128, 0:2048],
                                            in_=sgb[b][:, 0:2048], accum_op=ALU.add),
            deps=[sg, ("qOX", 16 * NTILE)], sem="qAC%d" % b, amt=16)
        ac2 = pr.add(
            "gpsimd",
            lambda e, b=b, t=t: e.dma_start(out=out_d[t * 128:(t + 1) * 128, 2048:4096],
                                            in_=sgb[b][:, 2048:4096], accum_op=ALU.add),
            deps=[sg], sem="qAC%d" % b, amt=16)
        sg_cons[b] = ac2

    if DEBUG:
        dbg = {}
        for nm, shape, dt in (("dbg_s", [128, NTILE * N], BF16), ("dbg_w", [128, 64], F32),
                              ("dbg_t8v", [128, 64], F32), ("dbg_idx", [128, 64], U32),
                              ("dbg_stats", [128, 16], F32), ("dbg_mb", [128, 1024], BF16),
                              ("dbg_rb", [128, 1024], BF16), ("dbg_g", [N, T], BF16)):
            dbg[nm] = nc.dram_tensor(nm, shape, dt, kind="ExternalOutput").ap()
        allv = [(s, v) for s, v in pr.cnt.items()]
        pr.add("sync", lambda e: e.dma_start(out=dbg["dbg_s"], in_=S_sb), deps=allv, sem="qS", amt=16)
        pr.add("sync", lambda e: e.dma_start(out=dbg["dbg_w"], in_=wsm), deps=allv, sem="qS", amt=16)
        pr.add("sync", lambda e: e.dma_start(out=dbg["dbg_t8v"], in_=t8v), deps=allv, sem="qS", amt=16)
        pr.add("sync", lambda e: e.dma_start(out=dbg["dbg_idx"], in_=idxw), deps=allv, sem="qS", amt=16)
        pr.add("sync", lambda e: e.dma_start(out=dbg["dbg_stats"], in_=stats), deps=allv, sem="qS", amt=16)
        pr.add("sync", lambda e: e.dma_start(out=dbg["dbg_mb"], in_=Mb), deps=allv, sem="qS", amt=16)
        pr.add("sync", lambda e: e.dma_start(out=dbg["dbg_rb"], in_=Rb), deps=allv, sem="qS", amt=16)
        pr.add("sync", lambda e: e.dma_start(out=dbg["dbg_g"], in_=g_hbm), deps=allv, sem="qS", amt=16)

    pr.add("sync", None, deps=[("qAC0", pr.cnt["qAC0"]), ("qAC1", pr.cnt["qAC1"]),
                               ("qOX", pr.cnt["qOX"])])

    with es:
        with nc.Block() as block:
            def emit(elist):
                def body(e):
                    for fn, waits, tok in elist:
                        for s in sorted(waits):
                            e.wait_ge(sems[s], waits[s])
                        if fn is None:
                            continue
                        ins = fn(e)
                        if tok is not None:
                            ins.then_inc(sems[tok[0]], 16 if tok[0].startswith("q") else 1)
                return body

            block.sync(emit(pr.items["sync"]))
            block.gpsimd(emit(pr.items["gpsimd"]))
            block.vector(emit(pr.items["vector"]))
            block.scalar(emit(pr.items["scalar"]))
            block.tensor(emit(pr.items["tensor"]))
    return nc


def kernel(L0_attn_x, L0_ffn_up, L0_ffn_down, L0_ffn_perm):
    global _NC_CACHE
    x = np.ascontiguousarray(np.asarray(L0_attn_x, np.float32).reshape(8192, N))
    up = np.asarray(L0_ffn_up, np.float32)
    down = np.asarray(L0_ffn_down, np.float32)
    perm = np.asarray(L0_ffn_perm).astype(np.int64)

    Fc_h, Fs_h, Cc_h, ident = _build_consts()
    up_h = np.ascontiguousarray(
        up.reshape(16, 2, 128, 1024).transpose(0, 2, 1, 3).reshape(16, 128, 2048))
    down_h = np.ascontiguousarray(
        down.reshape(16, 8, 128, 256).transpose(0, 2, 1, 3).reshape(16, 128, 2048))
    perm_t = np.ascontiguousarray(perm.reshape(32, 128).T).astype(np.uint32)
    rowb = ((np.arange(8)[None, :] * 128 + np.arange(128)[:, None]) * (2 * N)).astype(np.uint32)

    in_maps = []
    for c in range(NCORES):
        X = x[c * T:(c + 1) * T]
        Xt = np.ascontiguousarray(X.T)
        xf = np.zeros((NBIN, T), np.float32)
        xf[0:2049] = Xt[0:2049]
        xr = np.zeros((NBIN, T), np.float32)
        xr[0] = Xt[0]
        xr[1:2049] = Xt[4095:2047:-1]
        x_pfc = np.ascontiguousarray(Xt[perm])
        in_maps.append({
            "x_tok": X, "xf_in": xf, "xr_in": xr, "x_pf": x_pfc,
            "up_h": up_h, "down_h": down_h, "perm_t": perm_t, "rowb_in": rowb,
            "fc_in": Fc_h, "fs_in": Fs_h, "cc_in": Cc_h, "id_in": ident,
        })

    if _NC_CACHE is None:
        _NC_CACHE = build_nc()
    res = run_bass_kernel_spmd(_NC_CACHE, in_maps, core_ids=list(range(NCORES)))
    out = np.concatenate([np.asarray(r["out_d"], np.float32) for r in res.results], axis=0)
    return out.reshape(1, 8192, N)
